# version 02
# baseline (speedup 1.0000x reference)
import numpy as np
import ml_dtypes
from contextlib import ExitStack

import concourse.bass as bass
import concourse.mybir as mybir
from concourse.bass_utils import run_bass_kernel_spmd
from concourse.alu_op_type import AluOpType as OP

F32 = mybir.dt.float32
BF16 = mybir.dt.bfloat16
AF = mybir.ActivationFunctionType
AX = mybir.AxisListType

MAGIC = float(1.5 * 2.0 ** 23)
NSEQ, N, D, NH, NG, DH, HID = 2, 2048, 1024, 16, 4, 64, 4096
NTT = N // 128
NC_D = D // 128
NC_H = HID // 128
EPS1, EPS2, EPSQ = 1e-5, 1e-6, 1e-5


class Ctr:
    def __init__(self, handle):
        self.h = handle
        self.n = 0

    def inc(self, ins, k=1):
        ins.then_inc(self.h, k)
        self.n += k


class DmaCtr(Ctr):
    def inc(self, ins):
        ins.then_inc(self.h, 16)
        self.n += 1


def build_program(dbg=(), detect_races=True):
    nc = bass.Bass("TRN2", target_bir_lowering=False,
                   detect_race_conditions=detect_races)
    es = ExitStack()

    x_in = nc.dram_tensor("x", [NSEQ * N, D], F32, kind="ExternalInput").ap()
    wq_in = nc.dram_tensor("wq", [D, D], F32, kind="ExternalInput").ap()
    wkv_in = nc.dram_tensor("wkv", [D, 512], F32, kind="ExternalInput").ap()
    wout_in = nc.dram_tensor("wout", [D, D], F32, kind="ExternalInput").ap()
    w1t_in = nc.dram_tensor("w1t", [D, HID], F32, kind="ExternalInput").ap()
    w2t_in = nc.dram_tensor("w2t", [HID, D], F32, kind="ExternalInput").ap()
    ln2g_in = nc.dram_tensor("ln2g", [128, 8], F32, kind="ExternalInput").ap()
    ln2b_in = nc.dram_tensor("ln2b", [128, 8], F32, kind="ExternalInput").ap()
    qkn_in = nc.dram_tensor("qkn", [128, 4], F32, kind="ExternalInput").ap()
    idb_in = nc.dram_tensor("idb", [128, 128], BF16, kind="ExternalInput").ap()
    idf_in = nc.dram_tensor("idf", [128, 128], F32, kind="ExternalInput").ap()
    onesb_in = nc.dram_tensor("onesb", [1, N], BF16, kind="ExternalInput").ap()
    onesrowf_in = nc.dram_tensor("onesrowf", [1, 128], F32, kind="ExternalInput").ap()
    bo65_in = nc.dram_tensor("bo65", [128, 130], F32, kind="ExternalInput").ap()
    boq_b_in = nc.dram_tensor("boq_b", [128, 128], BF16, kind="ExternalInput").ap()
    boq_f_in = nc.dram_tensor("boq_f", [128, 128], F32, kind="ExternalInput").ap()
    bok_b_in = nc.dram_tensor("bok_b", [128, 8], BF16, kind="ExternalInput").ap()
    bok_f_in = nc.dram_tensor("bok_f", [128, 8], F32, kind="ExternalInput").ap()
    sel16_in = nc.dram_tensor("sel16", [16, 1024], F32, kind="ExternalInput").ap()
    selk_in = nc.dram_tensor("selk", [4, 256], F32, kind="ExternalInput").ap()
    onescolb_in = nc.dram_tensor("onescolb", [128, 1], BF16, kind="ExternalInput").ap()
    onescolf_in = nc.dram_tensor("onescolf", [128, 1], F32, kind="ExternalInput").ap()

    y_out = nc.dram_tensor("y", [NSEQ * N, D], F32, kind="ExternalOutput").ap()
    scr_x2 = nc.dram_tensor("scr_x2", [NSEQ * N, D], F32).ap()
    scr_w1t = nc.dram_tensor("scr_w1t", [NC_D, 128, HID], BF16).ap()
    scr_hq = nc.dram_tensor("scr_hq", [NC_H, 128, NSEQ * N], BF16).ap()

    DBG = {}
    for name, shape, dt in (
        ("xn", [128, NC_D * N], BF16), ("qln", [128, NC_D * N], BF16),
        ("kt65", [65, NG * N], BF16), ("vaug", [128, NG * 16 * 65], BF16),
        ("ao", [128, NC_D * N], BF16), ("yifm", [128, NC_D * NSEQ * N], BF16),
        ("c1c", [128, 32], F32), ("c2c", [128, 32], F32), ("s12", [128, 16], F32),
        ("pslab", [128, 16 * 1024], BF16), ("qt65", [65, N], BF16),
        ("qraw", [128, NC_D * N], BF16), ("kraw", [128, 2 * N], BF16),
        ("x2", [NSEQ * N, D], F32), ("hq", [NC_H, 128, NSEQ * N], BF16),
    ):
        if name in dbg:
            DBG[name] = nc.dram_tensor("dbg_" + name, shape, dt, kind="ExternalOutput").ap()

    dI = DmaCtr(es.enter_context(nc.semaphore("dI")))
    dIa = DmaCtr(es.enter_context(nc.semaphore("dIa")))
    dIb = DmaCtr(es.enter_context(nc.semaphore("dIb")))
    dO = DmaCtr(es.enter_context(nc.semaphore("dO")))
    dOa = DmaCtr(es.enter_context(nc.semaphore("dOa")))
    dOb = DmaCtr(es.enter_context(nc.semaphore("dOb")))
    DSEMS = (dI, dIa, dIb, dO, dOa, dOb)
    sP = Ctr(es.enter_context(nc.semaphore("sP")))
    sV = Ctr(es.enter_context(nc.semaphore("sV")))
    sA = Ctr(es.enter_context(nc.semaphore("sA")))

    blk = es.enter_context(nc.Block())
    _uid = [0]

    def sV_expect(q):
        return sV.n

    sSV = Ctr(es.enter_context(nc.semaphore("sSV")))
    sSA = Ctr(es.enter_context(nc.semaphore("sSA")))

    def VS(q, ins):
        sSV.inc(ins)
        q.wait_ge(sSV.h, sSV.n)

    def AS(q, ins):
        sSA.inc(ins)
        q.wait_ge(sSA.h, sSA.n)

    def _u(name):
        _uid[0] += 1
        return f"{name}_{_uid[0]}"

    def barrier():
        tp, tv, ta = sP.n, sV.n, sA.n
        dvals = [(c.h, 16 * c.n) for c in DSEMS]

        for eng in ("sync", "tensor", "vector", "scalar"):
            def body(q, eng=eng):
                if eng != "tensor":
                    q.wait_ge(sP.h, tp)
                if eng != "vector":
                    q.wait_ge(sV.h, tv)
                if eng != "scalar":
                    q.wait_ge(sA.h, ta)
                for h, v in dvals:
                    q.wait_ge(h, v)
            getattr(blk, eng)(body)

    def dbg_dump(name, src_ap):
        if name in DBG:
            barrier()

            @blk.sync
            def _(q):
                dI.inc(q.dma_start(out=DBG[name], in_=src_ap))
            barrier()

    idb = es.enter_context(nc.sbuf_tensor(_u("s_idb"), [128, 128], BF16))
    idf = es.enter_context(nc.sbuf_tensor(_u("s_idf"), [128, 128], F32))
    bo65 = es.enter_context(nc.sbuf_tensor(_u("s_bo65"), [128, 130], F32))
    boq_b = es.enter_context(nc.sbuf_tensor(_u("s_boq_b"), [128, 128], BF16))
    boq_f = es.enter_context(nc.sbuf_tensor(_u("s_boq_f"), [128, 128], F32))
    bok_b = es.enter_context(nc.sbuf_tensor(_u("s_bok_b"), [128, 8], BF16))
    bok_f = es.enter_context(nc.sbuf_tensor(_u("s_bok_f"), [128, 8], F32))
    sel16 = es.enter_context(nc.sbuf_tensor(_u("s_sel16"), [16, 1024], F32))
    selk = es.enter_context(nc.sbuf_tensor(_u("s_selk"), [4, 256], F32))
    ln2g = es.enter_context(nc.sbuf_tensor(_u("s_ln2g"), [128, 8], F32))
    ln2b = es.enter_context(nc.sbuf_tensor(_u("s_ln2b"), [128, 8], F32))
    qkn = es.enter_context(nc.sbuf_tensor(_u("s_qkn"), [128, 4], F32))
    onesrowf = es.enter_context(nc.sbuf_tensor(_u("s_onesrowf"), [1, 128], F32))
    onescolb = es.enter_context(nc.sbuf_tensor(_u("s_onescolb"), [128, 1], BF16))
    onescolf = es.enter_context(nc.sbuf_tensor(_u("s_onescolf"), [128, 1], F32))
    smisc = es.enter_context(nc.sbuf_tensor(_u("smisc"), [128, 16], F32))
    c1c = es.enter_context(nc.sbuf_tensor(_u("c1c"), [128, 32], F32))
    c2c = es.enter_context(nc.sbuf_tensor(_u("c2c"), [128, 32], F32))
    colmisc = es.enter_context(nc.sbuf_tensor(_u("colmisc"), [128, 16], F32))

    @blk.sync
    def _(q):
        for dst, src in (
            (idb[:], idb_in), (idf[:], idf_in), (bo65[:], bo65_in),
            (boq_b[:], boq_b_in), (boq_f[:], boq_f_in),
            (bok_b[:], bok_b_in), (bok_f[:], bok_f_in),
            (sel16[:], sel16_in), (selk[:], selk_in),
            (ln2g[:], ln2g_in), (ln2b[:], ln2b_in), (qkn[:], qkn_in),
            (onesrowf[:], onesrowf_in), (onescolb[:], onescolb_in),
            (onescolf[:], onescolf_in),
        ):
            dI.inc(q.dma_start(out=dst, in_=src))

    @blk.vector
    def _(q):
        for tns in (smisc, c1c, c2c, colmisc):
            q.memset(tns[:], 0.0)
        sV.inc(q.memset(mk2_init[:], 0.0)) if False else sV.inc(q.memset(colmisc[:], 0.0))

    awts = ExitStack()
    wq_b = awts.enter_context(nc.sbuf_tensor(_u("wq_b"), [128, NC_D * D], BF16))
    wkv_b = awts.enter_context(nc.sbuf_tensor(_u("wkv_b"), [128, NC_D * 512], BF16))
    wout_b = awts.enter_context(nc.sbuf_tensor(_u("wout_b"), [128, NC_D * D], BF16))

    def cast_weights():
        st = ExitStack()
        stage = st.enter_context(nc.sbuf_tensor(_u("wstage"), [128, 2 * 1024], F32))
        jobs = []
        for c in range(NC_D):
            jobs.append((wq_in[128 * c:128 * (c + 1), :], wq_b[:, D * c:D * (c + 1)], D))
            jobs.append((wout_in[128 * c:128 * (c + 1), :], wout_b[:, D * c:D * (c + 1)], D))
            jobs.append((wkv_in[128 * c:128 * (c + 1), :], wkv_b[:, 512 * c:512 * (c + 1)], 512))
        bia, bib, bv = dIa.n, dIb.n, sV.n

        @blk.sync
        def _(q):
            for i, (src, dst, w) in enumerate(jobs):
                if i >= 2:
                    q.wait_ge(sV.h, bv + i - 1)
                par = (dIa, dIb)[i % 2]
                par.inc(q.dma_start(out=stage[:, 1024 * (i % 2):1024 * (i % 2) + w], in_=src))

        @blk.vector
        def _(q):
            for i, (src, dst, w) in enumerate(jobs):
                par, base = ((dIa, bia), (dIb, bib))[i % 2]
                q.wait_ge(par.h, 16 * (base + i // 2 + 1))
                sV.inc(q.tensor_copy(out=dst, in_=stage[:, 1024 * (i % 2):1024 * (i % 2) + w]))
        barrier()
        st.close()

    cast_weights()

    def w_mean(w_in, rows, cols, scol, rscol, nscol):
        st = ExitStack()
        wst = st.enter_context(nc.sbuf_tensor(_u("wst"), [128, 2 * cols], F32))
        wab = st.enter_context(nc.sbuf_tensor(_u("wab"), [128, cols], F32))
        pss = st.enter_context(nc.psum_tensor(_u("pss"), [1, 512], F32))
        psb = st.enter_context(nc.psum_tensor(_u("psb"), [128, 1], F32))
        nslice = cols // 512

        for ph in range(2):
            bia, bib, bv, bp, ba_abs = dIa.n, dIb.n, sV.n, sP.n, sA.n

            @blk.sync
            def _(q, ph=ph, bia=bia, bib=bib, ba_abs=ba_abs):
                for r in range(rows):
                    if r >= 2:
                        q.wait_ge(sA.h, ba_abs + r - 1)
                    par = (dIa, dIb)[r % 2]
                    par.inc(q.dma_start(out=wst[:, cols * (r % 2):cols * (r % 2 + 1)],
                                        in_=w_in[128 * r:128 * (r + 1), :]))

            @blk.scalar
            def _(q, ph=ph, bia=bia, bib=bib, bp=bp, bv=bv):
                for r in range(rows):
                    par, base = ((dIa, bia), (dIb, bib))[r % 2]
                    q.wait_ge(par.h, 16 * (base + r // 2 + 1))
                    if r >= 1:
                        q.wait_ge(sP.h, bp + r)
                    if ph == 1 and r == 0:
                        q.wait_ge(sV.h, bv)
                    sA.inc(q.activation(out=wab[:], in_=wst[:, cols * (r % 2):cols * (r % 2 + 1)],
                                        func=AF.Abs))

            @blk.vector
            def _(q, ph=ph, ba_abs=ba_abs):
                if ph == 1:
                    for r in range(rows):
                        q.wait_ge(sA.h, ba_abs + r + 1)
                        sV.inc(q.tensor_scalar(out=wab[:], in0=wab[:], scalar1=scol[:, 0:1],
                                               scalar2=None, op0=OP.subtract))
            bv2 = sV.n

            @blk.tensor
            def _(q, ph=ph, ba_abs=ba_abs, bv2=bv2):
                for r in range(rows):
                    if ph == 0:
                        q.wait_ge(sA.h, ba_abs + r + 1)
                    else:
                        q.wait_ge(sV.h, bv2 - rows + r + 1)
                    for s in range(nslice):
                        ins = q.matmul(out=pss[:], lhsT=onescolf[:], rhs=wab[:, 512 * s:512 * (s + 1)],
                                       start=(r == 0 and s == 0),
                                       stop=(r == rows - 1 and s == nslice - 1))
                    sP.inc(ins)
            bp2 = sP.n

            @blk.vector
            def _(q, ph=ph, bp2=bp2):
                q.wait_ge(sP.h, bp2)
                VS(q, q.tensor_scalar(out=wst[0:1, 0:512], in0=pss[0:1, :], scalar1=1.0,
                                      scalar2=0.0, op0=OP.mult, op1=OP.add,
                                      accum_out=smisc[0:1, 9:10]))
                if ph == 0:
                    VS(q, q.tensor_scalar(out=scol[0:1, :], in0=smisc[0:1, 9:10],
                                          scalar1=float(1.0 / (rows * 128 * cols)),
                                          scalar2=None, op0=OP.mult))
                else:
                    VS(q, q.scalar_tensor_tensor(out=scol[0:1, :], in0=smisc[0:1, 9:10],
                                                 scalar=float(1.0 / (rows * 128 * cols)),
                                                 in1=scol[0:1, :], op0=OP.mult, op1=OP.add))
                    VS(q, q.tensor_scalar(out=scol[0:1, :], in0=scol[0:1, :],
                                          scalar1=float(EPSQ), scalar2=None, op0=OP.max))
                    VS(q, q.reciprocal(out=rscol[0:1, :], in_=scol[0:1, :]))
                    VS(q, q.tensor_scalar(out=nscol[0:1, :], in0=scol[0:1, :],
                                          scalar1=-1.0, scalar2=None, op0=OP.mult))
                sV.inc(q.tensor_copy(out=smisc[0:1, 10:11], in_=scol[0:1, :]))
            blist = (scol,) if ph == 0 else (scol, rscol, nscol)
            for srcb in blist:
                bv3 = sV.n

                @blk.tensor
                def _(q, srcb=srcb, bv3=bv3):
                    q.wait_ge(sV.h, bv3)
                    sP.inc(q.matmul(out=psb[:], lhsT=onesrowf[:], rhs=srcb[0:1, :],
                                    start=True, stop=True))

                @blk.vector
                def _(q, srcb=srcb, tgt=sP.n):
                    q.wait_ge(sP.h, tgt)
                    VS(q, q.tensor_copy(out=srcb, in_=psb[:]))
                    sV.inc(q.tensor_copy(out=smisc[0:1, 10:11], in_=srcb[0:1, :]))
            barrier()
        st.close()

    def w_tern(w_in, rows, cols, scol, rscol, nscol, scr=None, sbuf_dst=None):
        st = ExitStack()
        wst = st.enter_context(nc.sbuf_tensor(_u("wst2"), [128, 2 * cols], F32))
        wtmp = st.enter_context(nc.sbuf_tensor(_u("wtmp"), [128, cols], F32))
        werr = st.enter_context(nc.sbuf_tensor(_u("werr"), [128, cols], F32))
        if scr is not None:
            wtb = st.enter_context(nc.sbuf_tensor(_u("wtb"), [128, 2 * cols], BF16))
        bia, bib, boa, bob, bv = dIa.n, dIb.n, dOa.n, dOb.n, sV.n
        NOPS = 6

        @blk.sync
        def _(q):
            for r in range(rows):
                if r >= 2:
                    q.wait_ge(sV.h, bv + NOPS * (r - 1))
                par = (dIa, dIb)[r % 2]
                par.inc(q.dma_start(out=wst[:, cols * (r % 2):cols * (r % 2 + 1)],
                                    in_=w_in[128 * r:128 * (r + 1), :]))
                if scr is not None:
                    q.wait_ge(sV.h, bv + NOPS * (r + 1))
                    opar = (dOa, dOb)[r % 2]
                    opar.inc(q.dma_start(out=scr[r, :, :],
                                         in_=wtb[:, cols * (r % 2):cols * (r % 2 + 1)]))

        @blk.vector
        def _(q):
            for r in range(rows):
                par, base = ((dIa, bia), (dIb, bib))[r % 2]
                q.wait_ge(par.h, 16 * (base + r // 2 + 1))
                if scr is not None and r >= 2:
                    opar, obase = ((dOa, boa), (dOb, bob))[r % 2]
                    q.wait_ge(opar.h, 16 * (obase + r // 2))
                src = wst[:, cols * (r % 2):cols * (r % 2 + 1)]
                dst = (wtb[:, cols * (r % 2):cols * (r % 2 + 1)] if scr is not None
                       else sbuf_dst(r))
                sV.inc(q.tensor_scalar(out=wtmp[:], in0=src, scalar1=rscol[:, 0:1],
                                       scalar2=None, op0=OP.mult))
                sV.inc(q.scalar_tensor_tensor(out=werr[:], in0=wtmp[:], scalar=nscol[:, 0:1],
                                              in1=src, op0=OP.mult, op1=OP.add))
                sV.inc(q.scalar_tensor_tensor(out=wtmp[:], in0=werr[:], scalar=rscol[:, 0:1],
                                              in1=wtmp[:], op0=OP.mult, op1=OP.add))
                sV.inc(q.tensor_scalar(out=wtmp[:], in0=wtmp[:], scalar1=1.0, scalar2=MAGIC,
                                       op0=OP.mult, op1=OP.add))
                sV.inc(q.tensor_scalar(out=wtmp[:], in0=wtmp[:], scalar1=MAGIC, scalar2=1.0,
                                       op0=OP.subtract, op1=OP.min))
                sV.inc(q.tensor_scalar(out=dst, in0=wtmp[:], scalar1=-1.0, scalar2=None,
                                       op0=OP.max))
        barrier()
        st.close()

    w_mean(w1t_in, NC_D, HID, smisc[:, 0:1], smisc[:, 1:2], smisc[:, 4:5])
    w_tern(w1t_in, NC_D, HID, smisc[:, 0:1], smisc[:, 1:2], smisc[:, 4:5], scr=scr_w1t)
    w_mean(w2t_in, NC_H, D, smisc[:, 2:3], smisc[:, 3:4], smisc[:, 5:6])

    seqws = ExitStack()
    q_ln = seqws.enter_context(nc.sbuf_tensor(_u("q_ln"), [128, NC_D * N], BF16))
    kt65 = seqws.enter_context(nc.sbuf_tensor(_u("kt65"), [65, NG * N], BF16))
    vaug = seqws.enter_context(nc.sbuf_tensor(_u("vaug"), [128, NG * 16 * 65], BF16))
    ao_b = seqws.enter_context(nc.sbuf_tensor(_u("ao_b"), [128, NC_D * N], BF16))
    qsq = seqws.enter_context(nc.sbuf_tensor(_u("qsq"), [128, N], F32))
    qt65 = [seqws.enter_context(nc.sbuf_tensor(f"qt65_{i}", [65, N], BF16)) for i in range(2)]
    mk2 = seqws.enter_context(nc.sbuf_tensor(_u("mk2"), [128, 8], F32))
    xstage = seqws.enter_context(nc.sbuf_tensor(_u("xstage"), [128, 2 * D], F32))
    xn0 = seqws.enter_context(nc.sbuf_tensor(_u("xn0"), [128, 2 * D], BF16))

    @blk.sync
    def _(q):
        for g in range(NG):
            dI.inc(q.dma_start(out=kt65[64:65, N * g:N * (g + 1)], in_=onesb_in))

    @blk.vector
    def _(q):
        q.memset(mk2[:], 0.0)
        q.memset(qsq[:], 0.0)
        sV.inc(q.memset(vaug[:], 1.0))
    barrier()

    def ln12_phase(b, xn_b):
        st = ExitStack()
        pstr = st.enter_context(nc.psum_tensor(_u("ps_tr12"), [128, 2 * 1024], BF16))
        bia, bib, bv, bp, ba = dIa.n, dIb.n, sV.n, sP.n, sA.n

        @blk.sync
        def _(q):
            for t in range(NTT):
                if t >= 2:
                    q.wait_ge(sV.h, bv + 4 * (t - 1))
                    q.wait_ge(sA.h, ba + 2 * (t - 1))
                par = (dIa, dIb)[t % 2]
                par.inc(q.dma_start(out=xstage[:, D * (t % 2):D * (t % 2 + 1)],
                                    in_=x_in[N * b + 128 * t: N * b + 128 * (t + 1), :]))

        @blk.scalar
        def _(q):
            for t in range(NTT):
                par, base = ((dIa, bia), (dIb, bib))[t % 2]
                q.wait_ge(par.h, 16 * (base + t // 2 + 1))
                if t >= 1:
                    q.wait_ge(sV.h, bv + 4 * (t - 1) + 2)
                xs = xstage[:, D * (t % 2):D * (t % 2 + 1)]
                AS(q, q.activation(out=qsq[:, 1024:2048], in_=xs, func=AF.Square,
                                   accum_out=colmisc[:, 1:2]))
                sA.inc(q.copy(out=colmisc[:, 13:14], in_=colmisc[:, 1:2]))
                q.wait_ge(sV.h, bv + 4 * t + 2)
                AS(q, q.activation(out=colmisc[:, 4:5], in_=colmisc[:, 3:4], func=AF.Sqrt))
                sA.inc(q.copy(out=colmisc[:, 13:14], in_=colmisc[:, 4:5]))

        @blk.vector
        def _(q):
            for t in range(NTT):
                par, base = ((dIa, bia), (dIb, bib))[t % 2]
                q.wait_ge(par.h, 16 * (base + t // 2 + 1))
                xs = xstage[:, D * (t % 2):D * (t % 2 + 1)]
                VS(q, q.tensor_scalar(out=qsq[:, 0:1024], in0=xs, scalar1=1.0, scalar2=0.0,
                                      op0=OP.mult, op1=OP.add, accum_out=colmisc[:, 0:1]))
                VS(q, q.tensor_scalar(out=colmisc[:, 0:1], in0=colmisc[:, 0:1],
                                      scalar1=float(1.0 / D), scalar2=None, op0=OP.mult))
                sV.inc(q.tensor_copy(out=colmisc[:, 15:16], in_=colmisc[:, 0:1]))
                VS(q, q.tensor_tensor(out=colmisc[:, 2:3], in0=colmisc[:, 0:1],
                                      in1=colmisc[:, 0:1], op=OP.mult))
                q.wait_ge(sA.h, ba + 2 * t + 1)
                VS(q, q.scalar_tensor_tensor(out=colmisc[:, 3:4], in0=colmisc[:, 1:2],
                                             scalar=float(1.0 / D), in1=colmisc[:, 2:3],
                                             op0=OP.mult, op1=OP.subtract))
                VS(q, q.tensor_scalar(out=colmisc[:, 3:4], in0=colmisc[:, 3:4],
                                      scalar1=float(1.0 + EPS2), scalar2=float(EPS1 * EPS2),
                                      op0=OP.mult, op1=OP.add))
                sV.inc(q.tensor_copy(out=colmisc[:, 15:16], in_=colmisc[:, 3:4]))
                q.wait_ge(sA.h, ba + 2 * t + 2)
                VS(q, q.reciprocal(out=colmisc[:, 5:6], in_=colmisc[:, 4:5]))
                sV.inc(q.tensor_scalar(out=xn0[:, D * (t % 2):D * (t % 2 + 1)], in0=xs,
                                       scalar1=colmisc[:, 0:1], scalar2=colmisc[:, 5:6],
                                       op0=OP.subtract, op1=OP.mult))
                q.wait_ge(sP.h, bp + t + 1)
                for c in range(NC_D):
                    pslc = pstr[:, 1024 * (t % 2) + 128 * c: 1024 * (t % 2) + 128 * (c + 1)]
                    ins = q.tensor_scalar(out=xn_b[:, N * c + 128 * t: N * c + 128 * (t + 1)],
                                          in0=pslc, scalar1=ln2g[:, c:c + 1],
                                          scalar2=ln2b[:, c:c + 1], op0=OP.mult, op1=OP.add)
                VS(q, ins)
                sV.inc(q.tensor_copy(out=colmisc[:, 15:16], in_=colmisc[:, 0:1]))

        @blk.tensor
        def _(q):
            for t in range(NTT):
                q.wait_ge(sV.h, bv + 4 * t + 3)
                if t >= 2:
                    q.wait_ge(sV.h, bv + 4 * (t - 1))
                for c in range(NC_D):
                    ins = q.transpose(
                        out=pstr[:, 1024 * (t % 2) + 128 * c: 1024 * (t % 2) + 128 * (c + 1)],
                        in_=xn0[:, D * (t % 2) + 128 * c: D * (t % 2) + 128 * (c + 1)],
                        identity=idb[:])
                sP.inc(ins)
        barrier()
        st.close()

    def proj_phase(b, xn_b):
        st = ExitStack()
        psq = st.enter_context(nc.psum_tensor(_u("ps_proj"), [128, 2 * 512], F32))
        bp, ba = sP.n, sA.n
        jobs = ([("q", m, tb) for m in range(NC_D) for tb in range(4)]
                + [("k", kc, tb) for kc in range(2) for tb in range(4)]
                + [("v", t, 0) for t in range(NTT)])

        @blk.tensor
        def _(q):
            for j, (kind, a1, a2) in enumerate(jobs):
                if j >= 2:
                    q.wait_ge(sA.h, ba + j - 1)
                half = 512 * (j % 2)
                if kind == "q":
                    m, tb = a1, a2
                    for ic in range(NC_D):
                        ins = q.matmul(out=psq[:, half:half + 512],
                                       lhsT=wq_b[:, D * ic + 128 * m: D * ic + 128 * (m + 1)],
                                       rhs=xn_b[:, N * ic + 512 * tb: N * ic + 512 * (tb + 1)],
                                       start=(ic == 0), stop=(ic == NC_D - 1))
                elif kind == "k":
                    kc, tb = a1, a2
                    for ic in range(NC_D):
                        ins = q.matmul(out=psq[:, half:half + 512],
                                       lhsT=wkv_b[:, 512 * ic + 128 * kc: 512 * ic + 128 * (kc + 1)],
                                       rhs=xn_b[:, N * ic + 512 * tb: N * ic + 512 * (tb + 1)],
                                       start=(ic == 0), stop=(ic == NC_D - 1))
                else:
                    t = a1
                    for ic in range(NC_D):
                        ins = q.matmul(out=psq[:, half:half + 256],
                                       lhsT=xn_b[:, N * ic + 128 * t: N * ic + 128 * (t + 1)],
                                       rhs=wkv_b[:, 512 * ic + 256: 512 * (ic + 1)],
                                       start=(ic == 0), stop=(ic == NC_D - 1))
                sP.inc(ins)

        @blk.scalar
        def _(q):
            for j, (kind, a1, a2) in enumerate(jobs):
                q.wait_ge(sP.h, bp + j + 1)
                half = 512 * (j % 2)
                if kind == "q":
                    m, tb = a1, a2
                    ins = q.copy(out=q_ln[:, N * m + 512 * tb: N * m + 512 * (tb + 1)],
                                 in_=psq[:, half:half + 512])
                elif kind == "k":
                    kc, tb = a1, a2
                    ins = q.copy(out=kraw[:, N * kc + 512 * tb: N * kc + 512 * (tb + 1)],
                                 in_=psq[:, half:half + 512])
                else:
                    t = a1
                    for g in range(NG):
                        ins = q.copy(out=vaug[:, 1040 * g + 65 * t: 1040 * g + 65 * t + 64],
                                     in_=psq[:, half + 64 * g: half + 64 * (g + 1)])
                sA.inc(ins)
        barrier()
        st.close()

    def headln_phase(b, nhd, ncnk, raw_get, bo_bf, bo_ff, sel, gcol, bcol, write_out):
        st = ExitStack()
        pst1 = st.enter_context(nc.psum_tensor(_u("ps_st1"), [nhd, 1024], F32))
        pst2 = st.enter_context(nc.psum_tensor(_u("ps_st2"), [nhd, 1024], F32))
        sums = st.enter_context(nc.sbuf_tensor(_u("hl_sums"), [nhd, N], F32))
        sumsq = st.enter_context(nc.sbuf_tensor(_u("hl_sumsq"), [nhd, N], F32))
        for H in range(2):
            ba, bp = sA.n, sP.n

            @blk.scalar
            def _(q, H=H):
                for c in range(ncnk):
                    q.wait_ge(sP.h, bp + c)
                    sA.inc(q.activation(out=qsq[:, 0:1024],
                                        in_=raw_get(c)[:, 1024 * H:1024 * (H + 1)],
                                        func=AF.Square))

            @blk.tensor
            def _(q, H=H):
                for c in range(ncnk):
                    q.wait_ge(sA.h, ba + c + 1)
                    for s in range(2):
                        q.matmul(out=pst1[:, 512 * s:512 * (s + 1)],
                                 lhsT=bo_bf[:, nhd * c:nhd * (c + 1)],
                                 rhs=raw_get(c)[:, 1024 * H + 512 * s:1024 * H + 512 * (s + 1)],
                                 start=(c == 0), stop=(c == ncnk - 1))
                    for s in range(2):
                        ins = q.matmul(out=pst2[:, 512 * s:512 * (s + 1)],
                                       lhsT=bo_ff[:, nhd * c:nhd * (c + 1)],
                                       rhs=qsq[:, 512 * s:512 * (s + 1)],
                                       start=(c == 0), stop=(c == ncnk - 1))
                    sP.inc(ins)

            @blk.vector
            def _(q, H=H, tgt=sP.n):
                q.wait_ge(sP.h, tgt)
                q.tensor_copy(out=sums[:, 1024 * H:1024 * (H + 1)], in_=pst1[:])
                sV.inc(q.tensor_copy(out=sumsq[:, 1024 * H:1024 * (H + 1)], in_=pst2[:]))

            @blk.tensor
            def _(q, tgt=sV.n):
                q.wait_ge(sV.h, tgt)
        bv, ba = sV.n, sA.n

        @blk.vector
        def _(q):
            q.tensor_scalar(out=sums[:], in0=sums[:], scalar1=float(1.0 / DH), scalar2=None,
                            op0=OP.mult)
            q.tensor_scalar(out=sumsq[:], in0=sumsq[:], scalar1=float(1.0 / DH), scalar2=None,
                            op0=OP.mult)
            q.tensor_tensor(out=qsq[0:nhd, 0:N], in0=sums[:], in1=sums[:], op=OP.mult)
            q.tensor_tensor(out=sumsq[:], in0=sumsq[:], in1=qsq[0:nhd, 0:N], op=OP.subtract)
            sV.inc(q.tensor_scalar(out=sumsq[:], in0=sumsq[:], scalar1=1.0, scalar2=float(EPS2),
                                   op0=OP.mult, op1=OP.add))

        @blk.scalar
        def _(q):
            q.wait_ge(sV.h, bv + 1)
            sA.inc(q.activation(out=sumsq[:], in_=sumsq[:], func=AF.Sqrt))

        @blk.vector
        def _(q):
            q.wait_ge(sA.h, ba + 1)
            VS(q, q.reciprocal(out=sumsq[:], in_=sumsq[:]))
            sV.inc(q.tensor_copy(out=qsq[0:nhd, 0:N], in_=sumsq[:]))
        barrier()

        st2 = ExitStack()
        psmu = st2.enter_context(nc.psum_tensor(_u("ps_mu"), [128, 1024], F32))
        psrs = st2.enter_context(nc.psum_tensor(_u("ps_rs"), [128, 1024], F32))
        qtmp = st2.enter_context(nc.sbuf_tensor(_u("hl_tmp"), [128, 1024], F32))
        for c in range(ncnk):
            for H in range(2):
                bp3 = sP.n

                @blk.tensor
                def _(q, c=c, H=H, bp3=bp3):
                    for s in range(2):
                        q.matmul(out=psmu[:, 512 * s:512 * (s + 1)],
                                 lhsT=sel[:, 128 * c:128 * (c + 1)],
                                 rhs=sums[:, 1024 * H + 512 * s:1024 * H + 512 * (s + 1)],
                                 start=True, stop=True)
                        ins = q.matmul(out=psrs[:, 512 * s:512 * (s + 1)],
                                       lhsT=sel[:, 128 * c:128 * (c + 1)],
                                       rhs=sumsq[:, 1024 * H + 512 * s:1024 * H + 512 * (s + 1)],
                                       start=True, stop=True)
                    sP.inc(ins)

                @blk.vector
                def _(q, c=c, H=H, tgt=sP.n):
                    q.wait_ge(sP.h, tgt)
                    q.tensor_tensor(out=qtmp[:], in0=raw_get(c)[:, 1024 * H:1024 * (H + 1)],
                                    in1=psmu[:], op=OP.subtract)
                    q.tensor_tensor(out=qtmp[:], in0=qtmp[:], in1=psrs[:], op=OP.mult)
                    write_out(q, c, H, qtmp)

                @blk.tensor
                def _(q, tgt=sV.n):
                    q.wait_ge(sV.h, tgt)
        barrier()
        st2.close()
        st.close()

    def headln_q(b):
        def write_out(q, c, H, qtmp):
            sl = slice(N * c + 1024 * H, N * c + 1024 * (H + 1))
            sV.inc(q.tensor_scalar(out=q_ln[:, sl], in0=qtmp[:],
                                   scalar1=qkn[:, 0:1], scalar2=qkn[:, 1:2],
                                   op0=OP.mult, op1=OP.add))
        headln_phase(b, 16, NC_D, lambda c: q_ln[:, N * c:N * (c + 1)],
                     boq_b, boq_f, sel16, qkn[:, 0:1], qkn[:, 1:2], write_out)

    def headln_k(b):
        def write_out(q, c, H, qtmp):
            gsl = slice(N * (2 * c) + 1024 * H, N * (2 * c) + 1024 * (H + 1))
            gsl2 = slice(N * (2 * c + 1) + 1024 * H, N * (2 * c + 1) + 1024 * (H + 1))
            q.tensor_scalar(out=kt65[0:64, gsl], in0=qtmp[0:64, :],
                            scalar1=qkn[0:64, 2:3], scalar2=qkn[0:64, 3:4],
                            op0=OP.mult, op1=OP.add)
            sV.inc(q.tensor_scalar(out=kt65[0:64, gsl2], in0=qtmp[64:128, :],
                                   scalar1=qkn[64:128, 2:3], scalar2=qkn[64:128, 3:4],
                                   op0=OP.mult, op1=OP.add))
        headln_phase(b, 4, 2, lambda c: kraw[:, N * c:N * (c + 1)],
                     bok_b, bok_f, selk, qkn[:, 2:3], qkn[:, 3:4], write_out)

    def mk2_phase(b):
        st = ExitStack()
        psn = st.enter_context(nc.psum_tensor(_u("ps_kn"), [65, 512], F32))
        for g in range(NG):
            ba4 = sA.n

            @blk.scalar
            def _(q, g=g, tgt=sP.n):
                q.wait_ge(sP.h, tgt)
                sA.inc(q.activation(out=qsq[0:64, 0:N], in_=kt65[0:64, N * g:N * (g + 1)],
                                    func=AF.Square))
            for quarter in range(4):
                bv4 = sV.n

                @blk.tensor
                def _(q, quarter=quarter, ba4=ba4, bv4=bv4):
                    q.wait_ge(sA.h, ba4 + 1)
                    if quarter >= 1:
                        q.wait_ge(sV.h, bv4)
                    sP.inc(q.matmul(out=psn[:], lhsT=bo65[0:64, 0:65],
                                    rhs=qsq[0:64, 512 * quarter:512 * (quarter + 1)],
                                    start=True, stop=True))

                @blk.vector
                def _(q, g=g, quarter=quarter, tgt=sP.n):
                    q.wait_ge(sP.h, tgt)
                    if quarter == 0:
                        VS(q, q.tensor_reduce(out=mk2[64:65, g:g + 1], in_=psn[64:65, :],
                                              axis=AX.X, op=OP.max))
                        sV.inc(q.tensor_copy(out=mk2[64:65, 6:7], in_=mk2[64:65, g:g + 1]))
                    else:
                        VS(q, q.tensor_reduce(out=mk2[64:65, 7:8], in_=psn[64:65, :],
                                              axis=AX.X, op=OP.max))
                        VS(q, q.tensor_tensor(out=mk2[64:65, g:g + 1], in0=mk2[64:65, g:g + 1],
                                              in1=mk2[64:65, 7:8], op=OP.max))
                        sV.inc(q.tensor_copy(out=mk2[64:65, 6:7], in_=mk2[64:65, g:g + 1]))

            @blk.vector
            def _(q, g=g):
                VS(q, q.tensor_scalar(out=mk2[64:65, g:g + 1], in0=mk2[64:65, g:g + 1],
                                      scalar1=64.0, scalar2=None, op0=OP.add))
                sV.inc(q.tensor_copy(out=mk2[64:65, 6:7], in_=mk2[64:65, g:g + 1]))
        barrier()
        st.close()

    def attn_core_phase(b, pslab, ao_tm):
        st = ExitStack()
        psS = [st.enter_context(nc.psum_tensor(_u(f"ps_S{i}"), [128, 1024], F32)) for i in range(2)]
        psPV = st.enter_context(nc.psum_tensor(_u("ps_PV"), [128, 65], F32))
        psn = st.enter_context(nc.psum_tensor(_u("ps_qn"), [65, 512], F32))
        psTR = st.enter_context(nc.psum_tensor(_u("ps_aTR"), [128, 1024], BF16))

        for c in range(NC_D):
            g = c // 2
            ba = sA.n

            @blk.scalar
            def _(q, c=c, tgt=sP.n):
                q.wait_ge(sP.h, tgt)
                sA.inc(q.activation(out=qsq[:], in_=q_ln[:, N * c:N * (c + 1)], func=AF.Square))

            for hp in range(2):
                hb = hp

                @blk.vector
                def _(q, c=c, hp=hp, hb=hb, tgt=sP.n):
                    q.wait_ge(sP.h, tgt)
                    sV.inc(q.tensor_scalar(out=qt65[hb][0:64, :],
                                           in0=q_ln[64 * hp:64 * (hp + 1), N * c:N * (c + 1)],
                                           scalar1=0.125, scalar2=None, op0=OP.mult))
                for quarter in range(4):
                    bv4 = sV.n

                    @blk.tensor
                    def _(q, hp=hp, quarter=quarter, ba=ba, bv4=bv4):
                        q.wait_ge(sA.h, ba + 1)
                        if quarter >= 1:
                            q.wait_ge(sV.h, bv4)
                        sP.inc(q.matmul(out=psn[:], lhsT=bo65[:, 65 * hp:65 * (hp + 1)],
                                        rhs=qsq[:, 512 * quarter:512 * (quarter + 1)],
                                        start=True, stop=True))

                    @blk.vector
                    def _(q, g=g, hb=hb, quarter=quarter, tgt=sP.n):
                        q.wait_ge(sP.h, tgt)
                        sV.inc(q.tensor_scalar(
                            out=qt65[hb][64:65, 512 * quarter:512 * (quarter + 1)],
                            in0=psn[64:65, :], scalar1=mk2[64:65, g:g + 1],
                            scalar2=float(-1.0 / 16.0), op0=OP.add, op1=OP.mult))
                bvq = sV.n

                for qb in range(2):
                    bpS, baE = sP.n, sA.n

                    @blk.tensor
                    def _(q, g=g, hb=hb, qb=qb, bvq=bvq, baE=baE):
                        q.wait_ge(sV.h, bvq)
                        for kt in range(16):
                            if kt >= 2:
                                q.wait_ge(sA.h, baE + kt - 1)
                            for s in range(2):
                                ins = q.matmul(
                                    out=psS[kt % 2][:, 512 * s:512 * (s + 1)],
                                    lhsT=kt65[:, N * g + 128 * kt: N * g + 128 * (kt + 1)],
                                    rhs=qt65[hb][:, 1024 * qb + 512 * s: 1024 * qb + 512 * (s + 1)],
                                    start=True, stop=True)
                            sP.inc(ins)

                    @blk.scalar
                    def _(q, bpS=bpS):
                        for kt in range(16):
                            q.wait_ge(sP.h, bpS + kt + 1)
                            sA.inc(q.activation(out=pslab[:, 1024 * kt:1024 * (kt + 1)],
                                                in_=psS[kt % 2][:], func=AF.Exp))
                    baE2 = sA.n

                    for qt in range(8):
                        @blk.tensor
                        def _(q, g=g, qt=qt, baE2=baE2, tgtV=sV.n):
                            q.wait_ge(sA.h, baE2)
                            if qt >= 1:
                                q.wait_ge(sV.h, tgtV)
                            for kc in range(16):
                                ins = q.matmul(
                                    out=psPV[:],
                                    lhsT=pslab[:, 1024 * kc + 128 * qt: 1024 * kc + 128 * (qt + 1)],
                                    rhs=vaug[:, 1040 * g + 65 * kc: 1040 * g + 65 * (kc + 1)],
                                    start=(kc == 0), stop=(kc == 15))
                            sP.inc(ins)

                        @blk.vector
                        def _(q, hp=hp, qb=qb, qt=qt, tgt=sP.n):
                            q.wait_ge(sP.h, tgt)
                            VS(q, q.reciprocal(out=colmisc[:, 8:9], in_=psPV[:, 64:65]))
                            qtg = 8 * qb + qt
                            sV.inc(q.tensor_scalar(
                                out=ao_tm[:, 128 * qtg + 64 * hp: 128 * qtg + 64 * (hp + 1)],
                                in0=psPV[:, 0:64], scalar1=colmisc[:, 8:9],
                                scalar2=None, op0=OP.mult))

            for half in range(2):
                @blk.tensor
                def _(q, half=half, tgt=sV.n):
                    q.wait_ge(sV.h, tgt)
                    for j in range(8):
                        qtg = 8 * half + j
                        ins = q.transpose(out=psTR[:, 128 * j:128 * (j + 1)],
                                          in_=ao_tm[:, 128 * qtg:128 * (qtg + 1)],
                                          identity=idb[:])
                    sP.inc(ins)

                @blk.vector
                def _(q, c=c, half=half, tgt=sP.n):
                    q.wait_ge(sP.h, tgt)
                    sV.inc(q.tensor_copy(
                        out=ao_b[:, N * c + 1024 * half: N * c + 1024 * (half + 1)],
                        in_=psTR[:]))

                @blk.tensor
                def _(q, tgt=sV.n):
                    q.wait_ge(sV.h, tgt)
        barrier()
        st.close()

    def wout_phase(b):
        st = ExitStack()
        psW = st.enter_context(nc.psum_tensor(_u("ps_W"), [128, 2 * 512], F32))
        psT2 = st.enter_context(nc.psum_tensor(_u("ps_WT"), [128, 512], F32))
        wsc = st.enter_context(nc.sbuf_tensor(_u("wsc"), [128, 2 * 512], F32))
        x2asm = st.enter_context(nc.sbuf_tensor(_u("x2asm"), [128, 4 * 1024], F32))
        x2r = x2asm[:].rearrange("p (j w) -> p j w", w=1024)
        for tb in range(4):
            bp, ba, bv = sP.n, sA.n, sV.n
            for dc in range(NC_D):
                @blk.tensor
                def _(q, tb=tb, dc=dc, ba=ba):
                    if dc >= 2:
                        q.wait_ge(sA.h, ba + dc - 1)
                    half = 512 * (dc % 2)
                    for ic in range(NC_D):
                        ins = q.matmul(out=psW[:, half:half + 512],
                                       lhsT=wout_b[:, D * ic + 128 * dc: D * ic + 128 * (dc + 1)],
                                       rhs=ao_b[:, N * ic + 512 * tb: N * ic + 512 * (tb + 1)],
                                       start=(ic == 0), stop=(ic == NC_D - 1))
                    sP.inc(ins)

                @blk.scalar
                def _(q, dc=dc, tgt=sP.n):
                    q.wait_ge(sP.h, tgt)
                    half = 512 * (dc % 2)
                    sA.inc(q.copy(out=wsc[:, half:half + 512], in_=psW[:, half:half + 512]))

                @blk.tensor
                def _(q, dc=dc, tgt=sA.n, bv=bv):
                    q.wait_ge(sA.h, tgt)
                    if dc >= 1:
                        q.wait_ge(sV.h, bv + dc)
                    half = 512 * (dc % 2)
                    for j in range(4):
                        ins = q.transpose(out=psT2[:, 128 * j:128 * (j + 1)],
                                          in_=wsc[:, half + 128 * j: half + 128 * (j + 1)],
                                          identity=idf[:])
                    sP.inc(ins)

                @blk.vector
                def _(q, dc=dc, tgt=sP.n):
                    q.wait_ge(sP.h, tgt)
                    sV.inc(q.tensor_copy(
                        out=x2r[:, :, 128 * dc:128 * (dc + 1)],
                        in_=psT2[:].rearrange("p (j w) -> p j w", w=128)))
            bvr = sV.n
            bira, birb = dIa.n, dIb.n

            @blk.sync
            def _(q, b=b, tb=tb, bvr=bvr):
                for j in range(4):
                    t = 4 * tb + j
                    par = (dIa, dIb)[j % 2]
                    par.inc(q.dma_start(out=xstage[:, D * (j % 2):D * (j % 2 + 1)],
                                        in_=x_in[N * b + 128 * t: N * b + 128 * (t + 1), :]))
                    q.wait_ge(sV.h, bvr + j + 1)
                    dO.inc(q.dma_start(out=scr_x2[N * b + 128 * t: N * b + 128 * (t + 1), :],
                                       in_=x2asm[:, 1024 * j:1024 * (j + 1)]))

            @blk.vector
            def _(q, tb=tb, bira=bira, birb=birb):
                for j in range(4):
                    par, base = ((dIa, bira), (dIb, birb))[j % 2]
                    q.wait_ge(par.h, 16 * (base + j // 2 + 1))
                    sV.inc(q.tensor_tensor(out=x2asm[:, 1024 * j:1024 * (j + 1)],
                                           in0=x2asm[:, 1024 * j:1024 * (j + 1)],
                                           in1=xstage[:, D * (j % 2):D * (j % 2 + 1)],
                                           op=OP.add))
            barrier()
        st.close()

    kraw = None
    for b in range(NSEQ):
        krsc = ExitStack()
        kraw = krsc.enter_context(nc.sbuf_tensor(_u("kraw"), [128, 2 * N], BF16))
        xnsc = ExitStack()
        xn_b = xnsc.enter_context(nc.sbuf_tensor(_u("xn_b"), [128, NC_D * N], BF16))
        ln12_phase(b, xn_b)
        if b == 0:
            dbg_dump("xn", xn_b[:])
        proj_phase(b, xn_b)
        xnsc.close()
        if b == 0:
            dbg_dump("qraw", q_ln[:])
            dbg_dump("kraw", kraw[:])
        headln_q(b)
        headln_k(b)
        mk2_phase(b)
        krsc.close()
        if b == 0:
            dbg_dump("qln", q_ln[:])
            dbg_dump("kt65", kt65[:])
            dbg_dump("vaug", vaug[:])
        core_sc = ExitStack()
        pslab = core_sc.enter_context(nc.sbuf_tensor(_u("pslab"), [128, 16 * 1024], BF16))
        ao_tm = core_sc.enter_context(nc.sbuf_tensor(_u("ao_tm"), [128, 16 * 128], BF16))
        attn_core_phase(b, pslab, ao_tm)
        if b == 0:
            dbg_dump("ao", ao_b[:])
        core_sc.close()
        wout_phase(b)

    dbg_dump("x2", scr_x2)
    seqws.close()
    awts.close()

    ffn = ExitStack()
    yi_fm = ffn.enter_context(nc.sbuf_tensor(_u("yi_fm"), [128, NC_D * NSEQ * N], BF16))
    NT_ALL = NSEQ * NTT

    def ln3_quant_phase():
        st = ExitStack()
        pstr = st.enter_context(nc.psum_tensor(_u("ps_tr3"), [128, 2 * 1024], BF16))
        xst = st.enter_context(nc.sbuf_tensor(_u("f_xst"), [128, 2 * D], F32))
        ytmp = st.enter_context(nc.sbuf_tensor(_u("f_ytmp"), [128, D], F32))
        yjunk = st.enter_context(nc.sbuf_tensor(_u("f_yjunk"), [128, 2 * D], F32))
        yib = st.enter_context(nc.sbuf_tensor(_u("f_yib"), [128, 2 * D], BF16))
        bia, bib, bv, bp, ba = dIa.n, dIb.n, sV.n, sP.n, sA.n
        yir = yi_fm[:].rearrange("p (c w) -> p c w", w=NSEQ * N)

        @blk.sync
        def _(q):
            for t in range(NT_ALL):
                if t >= 2:
                    q.wait_ge(sV.h, bv + 6 * (t - 1))
                    q.wait_ge(sA.h, ba + 2 * (t - 1))
                par = (dIa, dIb)[t % 2]
                par.inc(q.dma_start(out=xst[:, D * (t % 2):D * (t % 2 + 1)],
                                   in_=scr_x2[128 * t:128 * (t + 1), :]))

        @blk.scalar
        def _(q):
            for t in range(NT_ALL):
                par, base = ((dIa, bia), (dIb, bib))[t % 2]
                q.wait_ge(par.h, 16 * (base + t // 2 + 1))
                if t >= 1:
                    q.wait_ge(sV.h, bv + 6 * (t - 1) + 2)
                xs = xst[:, D * (t % 2):D * (t % 2 + 1)]
                AS(q, q.activation(out=yjunk[:, D:2 * D], in_=xs, func=AF.Square,
                                   accum_out=colmisc[:, 1:2]))
                sA.inc(q.copy(out=colmisc[:, 13:14], in_=colmisc[:, 1:2]))
                q.wait_ge(sV.h, bv + 6 * t + 2)
                AS(q, q.activation(out=colmisc[:, 4:5], in_=colmisc[:, 3:4], func=AF.Sqrt))
                sA.inc(q.copy(out=colmisc[:, 13:14], in_=colmisc[:, 4:5]))

        @blk.vector
        def _(q):
            for t in range(NT_ALL):
                par, base = ((dIa, bia), (dIb, bib))[t % 2]
                q.wait_ge(par.h, 16 * (base + t // 2 + 1))
                xs = xst[:, D * (t % 2):D * (t % 2 + 1)]
                VS(q, q.tensor_scalar(out=yjunk[:, 0:D], in0=xs, scalar1=1.0, scalar2=0.0,
                                      op0=OP.mult, op1=OP.add, accum_out=colmisc[:, 0:1]))
                VS(q, q.tensor_scalar(out=colmisc[:, 0:1], in0=colmisc[:, 0:1],
                                      scalar1=float(1.0 / D), scalar2=None, op0=OP.mult))
                sV.inc(q.tensor_copy(out=colmisc[:, 15:16], in_=colmisc[:, 0:1]))
                VS(q, q.tensor_tensor(out=colmisc[:, 2:3], in0=colmisc[:, 0:1],
                                      in1=colmisc[:, 0:1], op=OP.mult))
                q.wait_ge(sA.h, ba + 2 * t + 1)
                VS(q, q.scalar_tensor_tensor(out=colmisc[:, 3:4], in0=colmisc[:, 1:2],
                                             scalar=float(1.0 / D), in1=colmisc[:, 2:3],
                                             op0=OP.mult, op1=OP.subtract))
                VS(q, q.tensor_scalar(out=colmisc[:, 3:4], in0=colmisc[:, 3:4],
                                      scalar1=1.0, scalar2=float(EPS1),
                                      op0=OP.mult, op1=OP.add))
                sV.inc(q.tensor_copy(out=colmisc[:, 15:16], in_=colmisc[:, 3:4]))
                q.wait_ge(sA.h, ba + 2 * t + 2)
                VS(q, q.reciprocal(out=colmisc[:, 5:6], in_=colmisc[:, 4:5]))
                sV.inc(q.tensor_scalar(out=ytmp[:], in0=xs,
                                       scalar1=colmisc[:, 0:1], scalar2=None, op0=OP.subtract))
                VS(q, q.tensor_reduce(out=colmisc[:, 6:7], in_=ytmp[:], axis=AX.X, op=OP.max,
                                      apply_absolute_value=True))
                VS(q, q.tensor_tensor(out=colmisc[:, 6:7], in0=colmisc[:, 6:7],
                                      in1=colmisc[:, 5:6], op=OP.mult))
                VS(q, q.tensor_scalar(out=colmisc[:, 6:7], in0=colmisc[:, 6:7],
                                      scalar1=float(EPSQ), scalar2=None, op0=OP.max))
                q.scalar_tensor_tensor(out=c1c[:, t:t + 1], in0=colmisc[:, 6:7],
                                       scalar=float(1.0 / 127.0), in1=smisc[:, 0:1],
                                       op0=OP.mult, op1=OP.mult)
                VS(q, q.reciprocal(out=colmisc[:, 7:8], in_=colmisc[:, 6:7]))
                VS(q, q.tensor_scalar(out=colmisc[:, 7:8], in0=colmisc[:, 7:8],
                                      scalar1=127.0, scalar2=None, op0=OP.mult))
                VS(q, q.tensor_tensor(out=colmisc[:, 7:8], in0=colmisc[:, 7:8],
                                      in1=colmisc[:, 5:6], op=OP.mult))
                sV.inc(q.tensor_copy(out=colmisc[:, 15:16], in_=colmisc[:, 7:8]))
                q.tensor_scalar(out=ytmp[:], in0=ytmp[:], scalar1=colmisc[:, 7:8],
                                scalar2=MAGIC, op0=OP.mult, op1=OP.add)
                sV.inc(q.tensor_scalar(out=yib[:, D * (t % 2):D * (t % 2 + 1)], in0=ytmp[:],
                                       scalar1=MAGIC, scalar2=None, op0=OP.subtract))
                q.wait_ge(sP.h, bp + t + 1)
                ins = q.tensor_copy(out=yir[:, :, 128 * t:128 * (t + 1)],
                                    in_=pstr[:, 1024 * (t % 2):1024 * (t % 2 + 1)]
                                    .rearrange("p (c w) -> p c w", w=128))
                VS(q, ins)
                sV.inc(q.tensor_copy(out=colmisc[:, 15:16], in_=colmisc[:, 0:1]))

        @blk.tensor
        def _(q):
            for t in range(NT_ALL):
                q.wait_ge(sV.h, bv + 6 * t + 5)
                if t >= 2:
                    q.wait_ge(sV.h, bv + 6 * (t - 1))
                for c in range(NC_D):
                    ins = q.transpose(
                        out=pstr[:, 1024 * (t % 2) + 128 * c: 1024 * (t % 2) + 128 * (c + 1)],
                        in_=yib[:, D * (t % 2) + 128 * c: D * (t % 2) + 128 * (c + 1)],
                        identity=idb[:])
                sP.inc(ins)
        barrier()
        st.close()

    def ffn1_phase():
        st = ExitStack()
        psF = st.enter_context(nc.psum_tensor(_u("ps_F1"), [128, 2 * 512], F32))
        pstr = st.enter_context(nc.psum_tensor(_u("ps_trH"), [128, 1024], BF16))
        w1tb = st.enter_context(nc.sbuf_tensor(_u("w1tb"), [128, 2 * HID], BF16))
        htile = st.enter_context(nc.sbuf_tensor(_u("htile"), [128, 4 * HID], F32))
        hqtm = st.enter_context(nc.sbuf_tensor(_u("hqtm"), [128, HID], BF16))
        hqfm = st.enter_context(nc.sbuf_tensor(_u("hqfm"), [128, NC_H * 512], BF16))
        hqr = hqfm[:].rearrange("p (c w) -> p c w", w=512)
        for bl in range(8):
            bia, bib, bv, bp, ba = dIa.n, dIb.n, sV.n, sP.n, sA.n
            for hb in range(8):
                buf = hb % 2

                @blk.sync
                def _(q, bl=bl, hb=hb, buf=buf, bp=bp):
                    if hb >= 2:
                        q.wait_ge(sP.h, bp + 4 * (hb - 1))
                    par = (dIa, dIb)[hb % 2]
                    for ic in range(NC_D):
                        par.inc(q.dma_start(
                            out=w1tb[:, HID * buf + 512 * ic: HID * buf + 512 * (ic + 1)],
                            in_=scr_w1t[ic, :, 512 * hb:512 * (hb + 1)]))

                for tt in range(4):
                    @blk.tensor
                    def _(q, bl=bl, hb=hb, tt=tt, buf=buf, bia=bia, bib=bib, bv=bv):
                        par, base = ((dIa, bia), (dIb, bib))[hb % 2]
                        q.wait_ge(par.h, 16 * (base + 8 * (hb // 2 + 1)))
                        jj = 4 * hb + tt
                        if jj >= 2:
                            q.wait_ge(sV.h, bv + jj - 1)
                        half = 512 * (jj % 2)
                        tg = 4 * bl + tt
                        for ic in range(NC_D):
                            ins = q.matmul(
                                out=psF[:, half:half + 512],
                                lhsT=yi_fm[:, NSEQ * N * ic + 128 * tg: NSEQ * N * ic + 128 * (tg + 1)],
                                rhs=w1tb[:, HID * buf + 512 * ic: HID * buf + 512 * (ic + 1)],
                                start=(ic == 0), stop=(ic == NC_D - 1))
                        sP.inc(ins)

                    @blk.vector
                    def _(q, bl=bl, hb=hb, tt=tt, tgt=sP.n):
                        q.wait_ge(sP.h, tgt)
                        jj = 4 * hb + tt
                        half = 512 * (jj % 2)
                        tg = 4 * bl + tt
                        sV.inc(q.tensor_scalar(
                            out=htile[:, HID * tt + 512 * hb: HID * tt + 512 * (hb + 1)],
                            in0=psF[:, half:half + 512], scalar1=c1c[:, tg:tg + 1],
                            scalar2=None, op0=OP.mult))
            for tt in range(4):
                tg = 4 * bl + tt
                bvg = sV.n

                @blk.scalar
                def _(q, tt=tt, tgt=sV.n):
                    q.wait_ge(sV.h, tgt)
                    sA.inc(q.activation(out=htile[:, HID * tt:HID * (tt + 1)],
                                        in_=htile[:, HID * tt:HID * (tt + 1)], func=AF.Gelu))

                @blk.vector
                def _(q, tt=tt, tg=tg, tgt=sA.n):
                    q.wait_ge(sA.h, tgt)
                    ht = htile[:, HID * tt:HID * (tt + 1)]
                    VS(q, q.tensor_reduce(out=colmisc[:, 6:7], in_=ht, axis=AX.X, op=OP.max,
                                          apply_absolute_value=True))
                    VS(q, q.tensor_scalar(out=colmisc[:, 6:7], in0=colmisc[:, 6:7],
                                          scalar1=float(EPSQ), scalar2=None, op0=OP.max))
                    q.scalar_tensor_tensor(out=c2c[:, tg:tg + 1], in0=colmisc[:, 6:7],
                                           scalar=float(1.0 / 127.0), in1=smisc[:, 2:3],
                                           op0=OP.mult, op1=OP.mult)
                    VS(q, q.reciprocal(out=colmisc[:, 7:8], in_=colmisc[:, 6:7]))
                    VS(q, q.tensor_scalar(out=colmisc[:, 7:8], in0=colmisc[:, 7:8],
                                          scalar1=127.0, scalar2=None, op0=OP.mult))
                    q.tensor_scalar(out=ht, in0=ht, scalar1=colmisc[:, 7:8],
                                    scalar2=MAGIC, op0=OP.mult, op1=OP.add)
                    sV.inc(q.tensor_scalar(out=hqtm[:], in0=ht, scalar1=MAGIC,
                                           scalar2=None, op0=OP.subtract))
                for rnd in range(4):
                    @blk.tensor
                    def _(q, tt=tt, rnd=rnd, tgt=sV.n):
                        q.wait_ge(sV.h, tgt)
                        for j in range(8):
                            hc = 8 * rnd + j
                            ins = q.transpose(out=pstr[:, 128 * j:128 * (j + 1)],
                                              in_=hqtm[:, 128 * hc:128 * (hc + 1)],
                                              identity=idb[:])
                        sP.inc(ins)

                    @blk.vector
                    def _(q, tt=tt, rnd=rnd, tgt=sP.n):
                        q.wait_ge(sP.h, tgt)
                        sV.inc(q.tensor_copy(
                            out=hqr[:, 8 * rnd:8 * (rnd + 1), 128 * tt:128 * (tt + 1)],
                            in_=pstr[:].rearrange("p (c w) -> p c w", w=128)))

                    @blk.tensor
                    def _(q, tgt=sV.n):
                        q.wait_ge(sV.h, tgt)
            bvS = sV.n

            @blk.sync
            def _(q, bl=bl, bvS=bvS):
                q.wait_ge(sV.h, bvS)
                for hc in range(NC_H):
                    dO.inc(q.dma_start(out=scr_hq[hc, :, 512 * bl:512 * (bl + 1)],
                                       in_=hqfm[:, 512 * hc:512 * (hc + 1)]))
            to = dO.n

            @blk.vector
            def _(q, to=to):
                q.wait_ge(dO.h, 16 * to)
        barrier()
        st.close()

    def ffn2_phase():
        st = ExitStack()
        w2tb = st.enter_context(nc.sbuf_tensor(_u("w2tb"), [128, NC_H * 1024], BF16))
        w_tern(w2t_in, NC_H, D, smisc[:, 2:3], smisc[:, 3:4], smisc[:, 5:6],
               sbuf_dst=lambda r: w2tb[:, 1024 * r:1024 * (r + 1)])

        psF = st.enter_context(nc.psum_tensor(_u("ps_F2"), [128, 2 * 512], F32))
        psT2 = st.enter_context(nc.psum_tensor(_u("ps_F2T"), [128, 512], F32))
        hqin = st.enter_context(nc.sbuf_tensor(_u("hqin"), [128, NC_H * 512], BF16))
        wsc = st.enter_context(nc.sbuf_tensor(_u("f2wsc"), [128, 2 * 512], F32))
        oasm = st.enter_context(nc.sbuf_tensor(_u("oasm"), [128, 4 * 1024], F32))
        xst = st.enter_context(nc.sbuf_tensor(_u("f2xst"), [128, 2 * D], F32))
        oar = oasm[:].rearrange("p (j w) -> p j w", w=1024)
        for bl in range(8):
            bi = dI.n

            @blk.sync
            def _(q, bl=bl):
                for hc in range(NC_H):
                    dI.inc(q.dma_start(out=hqin[:, 512 * hc:512 * (hc + 1)],
                                       in_=scr_hq[hc, :, 512 * bl:512 * (bl + 1)]))
            bp, ba, bv = sP.n, sA.n, sV.n
            for dc in range(NC_D):
                @blk.tensor
                def _(q, dc=dc, bi=bi, ba=ba):
                    q.wait_ge(dI.h, 16 * (bi + NC_H))
                    if dc >= 2:
                        q.wait_ge(sA.h, ba + dc - 1)
                    half = 512 * (dc % 2)
                    for hc in range(NC_H):
                        ins = q.matmul(out=psF[:, half:half + 512],
                                       lhsT=w2tb[:, 1024 * hc + 128 * dc: 1024 * hc + 128 * (dc + 1)],
                                       rhs=hqin[:, 512 * hc:512 * (hc + 1)],
                                       start=(hc == 0), stop=(hc == NC_H - 1))
                    sP.inc(ins)

                @blk.scalar
                def _(q, dc=dc, tgt=sP.n):
                    q.wait_ge(sP.h, tgt)
                    half = 512 * (dc % 2)
                    sA.inc(q.copy(out=wsc[:, half:half + 512], in_=psF[:, half:half + 512]))

                @blk.tensor
                def _(q, dc=dc, tgt=sA.n, bv=bv):
                    q.wait_ge(sA.h, tgt)
                    if dc >= 1:
                        q.wait_ge(sV.h, bv + dc)
                    half = 512 * (dc % 2)
                    for j in range(4):
                        ins = q.transpose(out=psT2[:, 128 * j:128 * (j + 1)],
                                          in_=wsc[:, half + 128 * j: half + 128 * (j + 1)],
                                          identity=idf[:])
                    sP.inc(ins)

                @blk.vector
                def _(q, dc=dc, tgt=sP.n):
                    q.wait_ge(sP.h, tgt)
                    sV.inc(q.tensor_copy(
                        out=oar[:, :, 128 * dc:128 * (dc + 1)],
                        in_=psT2[:].rearrange("p (j w) -> p j w", w=128)))
            bvr = sV.n
            bira, birb = dIa.n, dIb.n

            @blk.sync
            def _(q, bl=bl, bvr=bvr):
                for j in range(4):
                    t = 4 * bl + j
                    par = (dIa, dIb)[j % 2]
                    par.inc(q.dma_start(out=xst[:, D * (j % 2):D * (j % 2 + 1)],
                                        in_=scr_x2[128 * t:128 * (t + 1), :]))
                    q.wait_ge(sV.h, bvr + j + 1)
                    dO.inc(q.dma_start(out=y_out[128 * t:128 * (t + 1), :],
                                       in_=oasm[:, 1024 * j:1024 * (j + 1)]))

            @blk.vector
            def _(q, bl=bl, bira=bira, birb=birb):
                for j in range(4):
                    t = 4 * bl + j
                    par, base = ((dIa, bira), (dIb, birb))[j % 2]
                    q.wait_ge(par.h, 16 * (base + j // 2 + 1))
                    sV.inc(q.scalar_tensor_tensor(out=oasm[:, 1024 * j:1024 * (j + 1)],
                                                  in0=oasm[:, 1024 * j:1024 * (j + 1)],
                                                  scalar=c2c[:, t:t + 1],
                                                  in1=xst[:, D * (j % 2):D * (j % 2 + 1)],
                                                  op0=OP.mult, op1=OP.add))
            barrier()
        st.close()

    ln3_quant_phase()
    dbg_dump("yifm", yi_fm[:])
    dbg_dump("c1c", c1c[:])
    dbg_dump("s12", smisc[:])
    ffn1_phase()
    dbg_dump("hq", scr_hq)
    dbg_dump("c2c", c2c[:])
    ffn2_phase()
    ffn.close()

    dvals = [(c.h, 16 * c.n) for c in DSEMS]

    @blk.sync
    def _(q):
        for h, v in dvals:
            q.wait_ge(h, v)

    es.close()
    return nc, DBG


_ID128F = np.eye(128, dtype=np.float32)


def _consts():
    c = {}
    c["idf"] = _ID128F
    c["idb"] = _ID128F.astype(ml_dtypes.bfloat16)
    c["onesb"] = np.ones((1, N), dtype=ml_dtypes.bfloat16)
    c["onesrowf"] = np.ones((1, 128), dtype=np.float32)
    bo65 = np.zeros((128, 130), dtype=np.float32)
    bo65[0:64, 64] = 1.0
    bo65[64:128, 65 + 64] = 1.0
    c["bo65"] = bo65
    boq = np.zeros((128, 128), dtype=np.float32)
    for ch in range(8):
        for p in range(128):
            boq[p, 16 * ch + 2 * ch + p // 64] = 1.0
    c["boq_f"] = boq
    c["boq_b"] = boq.astype(ml_dtypes.bfloat16)
    bok = np.zeros((128, 8), dtype=np.float32)
    for ch in range(2):
        for p in range(128):
            bok[p, 4 * ch + 2 * ch + p // 64] = 1.0
    c["bok_f"] = bok
    c["bok_b"] = bok.astype(ml_dtypes.bfloat16)
    sel16 = np.zeros((16, 1024), dtype=np.float32)
    for ch in range(8):
        for p in range(128):
            sel16[2 * ch + p // 64, 128 * ch + p] = 1.0
    c["sel16"] = sel16
    selk = np.zeros((4, 256), dtype=np.float32)
    for ch in range(2):
        for p in range(128):
            selk[2 * ch + p // 64, 128 * ch + p] = 1.0
    c["selk"] = selk
    c["onescolb"] = np.ones((128, 1), dtype=ml_dtypes.bfloat16)
    c["onescolf"] = np.ones((128, 1), dtype=np.float32)
    return c


def make_inmaps(inputs):
    L = inputs
    x = np.ascontiguousarray(np.asarray(L["L0_attn_x"], dtype=np.float32))
    consts = _consts()
    shared = dict(
        wq=np.ascontiguousarray(np.asarray(L["L0_attn_Wq"], np.float32)),
        wkv=np.ascontiguousarray(np.asarray(L["L0_attn_Wkv"], np.float32)),
        wout=np.ascontiguousarray(np.asarray(L["L0_attn_Wout"], np.float32)),
        w1t=np.ascontiguousarray(np.asarray(L["L0_ffn_w1"], np.float32).T),
        w2t=np.ascontiguousarray(np.asarray(L["L0_ffn_w2"], np.float32).T),
        ln2g=np.ascontiguousarray(np.asarray(L["L0_attn_ln_scale"], np.float32).reshape(8, 128).T),
        ln2b=np.ascontiguousarray(np.asarray(L["L0_attn_ln_bias"], np.float32).reshape(8, 128).T),
        qkn=np.ascontiguousarray(np.stack([
            np.tile(np.asarray(L["L0_attn_qn_scale"], np.float32), 2),
            np.tile(np.asarray(L["L0_attn_qn_bias"], np.float32), 2),
            np.tile(np.asarray(L["L0_attn_kn_scale"], np.float32), 2),
            np.tile(np.asarray(L["L0_attn_kn_bias"], np.float32), 2)], axis=1)),
        **consts,
    )
    maps = []
    for core in range(8):
        m = dict(shared)
        m["x"] = np.ascontiguousarray(x[2 * core:2 * core + 2].reshape(NSEQ * N, D))
        maps.append(m)
    return maps


def kernel(**inputs):
    nc, _ = build_program()
    maps = make_inmaps(inputs)
    res = run_bass_kernel_spmd(nc, maps, core_ids=list(range(8)))
    out = np.empty((16, N, D), dtype=np.float32)
    for core in range(8):
        out[2 * core:2 * core + 2] = res.results[core]["y"].reshape(NSEQ, N, D)
    return out


# version 03
# speedup vs baseline: 1.0066x; 1.0066x over previous
import numpy as np
import ml_dtypes
from contextlib import ExitStack

import concourse.bass as bass
import concourse.mybir as mybir
from concourse.bass_utils import run_bass_kernel_spmd
from concourse.alu_op_type import AluOpType as OP

F32 = mybir.dt.float32
BF16 = mybir.dt.bfloat16
AF = mybir.ActivationFunctionType
AX = mybir.AxisListType

MAGIC = float(1.5 * 2.0 ** 23)
NSEQ, N, D, NH, NG, DH, HID = 2, 2048, 1024, 16, 4, 64, 4096
NTT = N // 128
NC_D = D // 128
NC_H = HID // 128
EPS1, EPS2, EPSQ = 1e-5, 1e-6, 1e-5


class Ctr:
    def __init__(self, handle):
        self.h = handle
        self.n = 0

    def inc(self, ins, k=1):
        ins.then_inc(self.h, k)
        self.n += k


class DmaCtr(Ctr):
    def inc(self, ins):
        ins.then_inc(self.h, 16)
        self.n += 1


def build_program(dbg=(), detect_races=True):
    nc = bass.Bass("TRN2", target_bir_lowering=False,
                   detect_race_conditions=detect_races)
    es = ExitStack()

    x_in = nc.dram_tensor("x", [NSEQ * N, D], F32, kind="ExternalInput").ap()
    wq_in = nc.dram_tensor("wq", [D, D], F32, kind="ExternalInput").ap()
    wkv_in = nc.dram_tensor("wkv", [D, 512], F32, kind="ExternalInput").ap()
    wout_in = nc.dram_tensor("wout", [D, D], F32, kind="ExternalInput").ap()
    w1t_in = nc.dram_tensor("w1t", [D, HID], F32, kind="ExternalInput").ap()
    w2t_in = nc.dram_tensor("w2t", [HID, D], F32, kind="ExternalInput").ap()
    ln2g_in = nc.dram_tensor("ln2g", [128, 8], F32, kind="ExternalInput").ap()
    ln2b_in = nc.dram_tensor("ln2b", [128, 8], F32, kind="ExternalInput").ap()
    qkn_in = nc.dram_tensor("qkn", [128, 4], F32, kind="ExternalInput").ap()
    idb_in = nc.dram_tensor("idb", [128, 128], BF16, kind="ExternalInput").ap()
    idf_in = nc.dram_tensor("idf", [128, 128], F32, kind="ExternalInput").ap()
    onesb_in = nc.dram_tensor("onesb", [1, N], BF16, kind="ExternalInput").ap()
    onesrowf_in = nc.dram_tensor("onesrowf", [1, 128], F32, kind="ExternalInput").ap()
    bo65_in = nc.dram_tensor("bo65", [128, 130], F32, kind="ExternalInput").ap()
    boq_b_in = nc.dram_tensor("boq_b", [128, 128], BF16, kind="ExternalInput").ap()
    boq_f_in = nc.dram_tensor("boq_f", [128, 128], F32, kind="ExternalInput").ap()
    bok_b_in = nc.dram_tensor("bok_b", [128, 8], BF16, kind="ExternalInput").ap()
    bok_f_in = nc.dram_tensor("bok_f", [128, 8], F32, kind="ExternalInput").ap()
    sel16_in = nc.dram_tensor("sel16", [16, 1024], F32, kind="ExternalInput").ap()
    selk_in = nc.dram_tensor("selk", [4, 256], F32, kind="ExternalInput").ap()
    onescolb_in = nc.dram_tensor("onescolb", [128, 1], BF16, kind="ExternalInput").ap()
    onescolf_in = nc.dram_tensor("onescolf", [128, 1], F32, kind="ExternalInput").ap()

    y_out = nc.dram_tensor("y", [NSEQ * N, D], F32, kind="ExternalOutput").ap()
    scr_x2 = nc.dram_tensor("scr_x2", [NSEQ * N, D], F32).ap()
    scr_w1t = nc.dram_tensor("scr_w1t", [NC_D, 128, HID], BF16).ap()
    scr_hq = nc.dram_tensor("scr_hq", [NC_H, 128, NSEQ * N], BF16).ap()

    DBG = {}
    for name, shape, dt in (
        ("xn", [128, NC_D * N], BF16), ("qln", [128, NC_D * N], BF16),
        ("kt65", [65, NG * N], BF16), ("vaug", [128, NG * 16 * 65], BF16),
        ("ao", [128, NC_D * N], BF16), ("yifm", [128, NC_D * NSEQ * N], BF16),
        ("c1c", [128, 32], F32), ("c2c", [128, 32], F32), ("s12", [128, 16], F32),
        ("pslab", [128, 16 * 1024], BF16), ("qt65", [65, N], BF16),
        ("qraw", [128, NC_D * N], BF16), ("kraw", [128, 2 * N], BF16),
        ("x2", [NSEQ * N, D], F32), ("hq", [NC_H, 128, NSEQ * N], BF16),
    ):
        if name in dbg:
            DBG[name] = nc.dram_tensor("dbg_" + name, shape, dt, kind="ExternalOutput").ap()

    dI = DmaCtr(es.enter_context(nc.semaphore("dI")))
    dIa = DmaCtr(es.enter_context(nc.semaphore("dIa")))
    dIb = DmaCtr(es.enter_context(nc.semaphore("dIb")))
    dO = DmaCtr(es.enter_context(nc.semaphore("dO")))
    dOa = DmaCtr(es.enter_context(nc.semaphore("dOa")))
    dOb = DmaCtr(es.enter_context(nc.semaphore("dOb")))
    DSEMS = (dI, dIa, dIb, dO, dOa, dOb)
    sP = Ctr(es.enter_context(nc.semaphore("sP")))
    sV = Ctr(es.enter_context(nc.semaphore("sV")))
    sA = Ctr(es.enter_context(nc.semaphore("sA")))

    blk = es.enter_context(nc.Block())
    _uid = [0]

    def sV_expect(q):
        return sV.n

    sSV = Ctr(es.enter_context(nc.semaphore("sSV")))
    sSA = Ctr(es.enter_context(nc.semaphore("sSA")))

    def VS(q, ins):
        sSV.inc(ins)
        q.wait_ge(sSV.h, sSV.n)

    def AS(q, ins):
        sSA.inc(ins)
        q.wait_ge(sSA.h, sSA.n)

    def _u(name):
        _uid[0] += 1
        return f"{name}_{_uid[0]}"

    def barrier():
        tp, tv, ta = sP.n, sV.n, sA.n
        dvals = [(c.h, 16 * c.n) for c in DSEMS]

        for eng in ("sync", "tensor", "vector", "scalar"):
            def body(q, eng=eng):
                if eng != "tensor":
                    q.wait_ge(sP.h, tp)
                if eng != "vector":
                    q.wait_ge(sV.h, tv)
                if eng != "scalar":
                    q.wait_ge(sA.h, ta)
                for h, v in dvals:
                    q.wait_ge(h, v)
            getattr(blk, eng)(body)

    def dbg_dump(name, src_ap):
        if name in DBG:
            barrier()

            @blk.sync
            def _(q):
                dI.inc(q.dma_start(out=DBG[name], in_=src_ap))
            barrier()

    idb = es.enter_context(nc.sbuf_tensor(_u("s_idb"), [128, 128], BF16))
    idf = es.enter_context(nc.sbuf_tensor(_u("s_idf"), [128, 128], F32))
    bo65 = es.enter_context(nc.sbuf_tensor(_u("s_bo65"), [128, 130], F32))
    boq_b = es.enter_context(nc.sbuf_tensor(_u("s_boq_b"), [128, 128], BF16))
    boq_f = es.enter_context(nc.sbuf_tensor(_u("s_boq_f"), [128, 128], F32))
    bok_b = es.enter_context(nc.sbuf_tensor(_u("s_bok_b"), [128, 8], BF16))
    bok_f = es.enter_context(nc.sbuf_tensor(_u("s_bok_f"), [128, 8], F32))
    sel16 = es.enter_context(nc.sbuf_tensor(_u("s_sel16"), [16, 1024], F32))
    selk = es.enter_context(nc.sbuf_tensor(_u("s_selk"), [4, 256], F32))
    ln2g = es.enter_context(nc.sbuf_tensor(_u("s_ln2g"), [128, 8], F32))
    ln2b = es.enter_context(nc.sbuf_tensor(_u("s_ln2b"), [128, 8], F32))
    qkn = es.enter_context(nc.sbuf_tensor(_u("s_qkn"), [128, 4], F32))
    onesrowf = es.enter_context(nc.sbuf_tensor(_u("s_onesrowf"), [1, 128], F32))
    onescolb = es.enter_context(nc.sbuf_tensor(_u("s_onescolb"), [128, 1], BF16))
    onescolf = es.enter_context(nc.sbuf_tensor(_u("s_onescolf"), [128, 1], F32))
    smisc = es.enter_context(nc.sbuf_tensor(_u("smisc"), [128, 16], F32))
    c1c = es.enter_context(nc.sbuf_tensor(_u("c1c"), [128, 32], F32))
    c2c = es.enter_context(nc.sbuf_tensor(_u("c2c"), [128, 32], F32))
    colmisc = es.enter_context(nc.sbuf_tensor(_u("colmisc"), [128, 16], F32))

    @blk.sync
    def _(q):
        for dst, src in (
            (idb[:], idb_in), (idf[:], idf_in), (bo65[:], bo65_in),
            (boq_b[:], boq_b_in), (boq_f[:], boq_f_in),
            (bok_b[:], bok_b_in), (bok_f[:], bok_f_in),
            (sel16[:], sel16_in), (selk[:], selk_in),
            (ln2g[:], ln2g_in), (ln2b[:], ln2b_in), (qkn[:], qkn_in),
            (onesrowf[:], onesrowf_in), (onescolb[:], onescolb_in),
            (onescolf[:], onescolf_in),
        ):
            dI.inc(q.dma_start(out=dst, in_=src))

    @blk.vector
    def _(q):
        for tns in (smisc, c1c, c2c):
            q.memset(tns[:], 0.0)
        sV.inc(q.memset(colmisc[:], 0.0))

    awts = ExitStack()
    wq_b = awts.enter_context(nc.sbuf_tensor(_u("wq_b"), [128, NC_D * D], BF16))
    wkv_b = awts.enter_context(nc.sbuf_tensor(_u("wkv_b"), [128, NC_D * 512], BF16))
    wout_b = awts.enter_context(nc.sbuf_tensor(_u("wout_b"), [128, NC_D * D], BF16))

    def cast_weights():
        st = ExitStack()
        stage = st.enter_context(nc.sbuf_tensor(_u("wstage"), [128, 2 * 1024], F32))
        jobs = []
        for c in range(NC_D):
            jobs.append((wq_in[128 * c:128 * (c + 1), :], wq_b[:, D * c:D * (c + 1)], D))
            jobs.append((wout_in[128 * c:128 * (c + 1), :], wout_b[:, D * c:D * (c + 1)], D))
            jobs.append((wkv_in[128 * c:128 * (c + 1), :], wkv_b[:, 512 * c:512 * (c + 1)], 512))
        bia, bib, bv = dIa.n, dIb.n, sV.n

        @blk.sync
        def _(q):
            for i, (src, dst, w) in enumerate(jobs):
                if i >= 2:
                    q.wait_ge(sV.h, bv + i - 1)
                par = (dIa, dIb)[i % 2]
                par.inc(q.dma_start(out=stage[:, 1024 * (i % 2):1024 * (i % 2) + w], in_=src))

        @blk.vector
        def _(q):
            for i, (src, dst, w) in enumerate(jobs):
                par, base = ((dIa, bia), (dIb, bib))[i % 2]
                q.wait_ge(par.h, 16 * (base + i // 2 + 1))
                sV.inc(q.tensor_copy(out=dst, in_=stage[:, 1024 * (i % 2):1024 * (i % 2) + w]))
        barrier()
        st.close()

    cast_weights()

    def w_mean(w_in, rows, cols, scol, rscol, nscol):
        st = ExitStack()
        wst = st.enter_context(nc.sbuf_tensor(_u("wst"), [128, 2 * cols], F32))
        wab = st.enter_context(nc.sbuf_tensor(_u("wab"), [128, cols], F32))
        pss = st.enter_context(nc.psum_tensor(_u("pss"), [1, 512], F32))
        psb = st.enter_context(nc.psum_tensor(_u("psb"), [128, 1], F32))
        nslice = cols // 512

        for ph in range(2):
            bia, bib, bv, bp, ba_abs = dIa.n, dIb.n, sV.n, sP.n, sA.n

            @blk.sync
            def _(q, ph=ph, bia=bia, bib=bib, ba_abs=ba_abs):
                for r in range(rows):
                    if r >= 2:
                        q.wait_ge(sA.h, ba_abs + r - 1)
                    par = (dIa, dIb)[r % 2]
                    par.inc(q.dma_start(out=wst[:, cols * (r % 2):cols * (r % 2 + 1)],
                                        in_=w_in[128 * r:128 * (r + 1), :]))

            @blk.scalar
            def _(q, ph=ph, bia=bia, bib=bib, bp=bp, bv=bv):
                for r in range(rows):
                    par, base = ((dIa, bia), (dIb, bib))[r % 2]
                    q.wait_ge(par.h, 16 * (base + r // 2 + 1))
                    if r >= 1:
                        q.wait_ge(sP.h, bp + r)
                    if ph == 1 and r == 0:
                        q.wait_ge(sV.h, bv)
                    sA.inc(q.activation(out=wab[:], in_=wst[:, cols * (r % 2):cols * (r % 2 + 1)],
                                        func=AF.Abs))

            @blk.vector
            def _(q, ph=ph, ba_abs=ba_abs):
                if ph == 1:
                    for r in range(rows):
                        q.wait_ge(sA.h, ba_abs + r + 1)
                        sV.inc(q.tensor_scalar(out=wab[:], in0=wab[:], scalar1=scol[:, 0:1],
                                               scalar2=None, op0=OP.subtract))
            bv2 = sV.n

            @blk.tensor
            def _(q, ph=ph, ba_abs=ba_abs, bv2=bv2):
                for r in range(rows):
                    if ph == 0:
                        q.wait_ge(sA.h, ba_abs + r + 1)
                    else:
                        q.wait_ge(sV.h, bv2 - rows + r + 1)
                    for s in range(nslice):
                        ins = q.matmul(out=pss[:], lhsT=onescolf[:], rhs=wab[:, 512 * s:512 * (s + 1)],
                                       start=(r == 0 and s == 0),
                                       stop=(r == rows - 1 and s == nslice - 1))
                    sP.inc(ins)
            bp2 = sP.n

            @blk.vector
            def _(q, ph=ph, bp2=bp2):
                q.wait_ge(sP.h, bp2)
                VS(q, q.tensor_scalar(out=wst[0:1, 0:512], in0=pss[0:1, :], scalar1=1.0,
                                      scalar2=0.0, op0=OP.mult, op1=OP.add,
                                      accum_out=smisc[0:1, 9:10]))
                if ph == 0:
                    VS(q, q.tensor_scalar(out=scol[0:1, :], in0=smisc[0:1, 9:10],
                                          scalar1=float(1.0 / (rows * 128 * cols)),
                                          scalar2=None, op0=OP.mult))
                else:
                    VS(q, q.scalar_tensor_tensor(out=scol[0:1, :], in0=smisc[0:1, 9:10],
                                                 scalar=float(1.0 / (rows * 128 * cols)),
                                                 in1=scol[0:1, :], op0=OP.mult, op1=OP.add))
                    VS(q, q.tensor_scalar(out=scol[0:1, :], in0=scol[0:1, :],
                                          scalar1=float(EPSQ), scalar2=None, op0=OP.max))
                    VS(q, q.reciprocal(out=rscol[0:1, :], in_=scol[0:1, :]))
                    VS(q, q.tensor_scalar(out=nscol[0:1, :], in0=scol[0:1, :],
                                          scalar1=-1.0, scalar2=None, op0=OP.mult))
                sV.inc(q.tensor_copy(out=smisc[0:1, 10:11], in_=scol[0:1, :]))
            blist = (scol,) if ph == 0 else (scol, rscol, nscol)
            for srcb in blist:
                bv3 = sV.n

                @blk.tensor
                def _(q, srcb=srcb, bv3=bv3):
                    q.wait_ge(sV.h, bv3)
                    sP.inc(q.matmul(out=psb[:], lhsT=onesrowf[:], rhs=srcb[0:1, :],
                                    start=True, stop=True))

                @blk.vector
                def _(q, srcb=srcb, tgt=sP.n):
                    q.wait_ge(sP.h, tgt)
                    VS(q, q.tensor_copy(out=srcb, in_=psb[:]))
                    sV.inc(q.tensor_copy(out=smisc[0:1, 10:11], in_=srcb[0:1, :]))
            barrier()
        st.close()

    def w_tern(w_in, rows, cols, scol, rscol, nscol, scr=None, sbuf_dst=None):
        st = ExitStack()
        wst = st.enter_context(nc.sbuf_tensor(_u("wst2"), [128, 2 * cols], F32))
        wtmp = st.enter_context(nc.sbuf_tensor(_u("wtmp"), [128, cols], F32))
        werr = st.enter_context(nc.sbuf_tensor(_u("werr"), [128, cols], F32))
        if scr is not None:
            wtb = st.enter_context(nc.sbuf_tensor(_u("wtb"), [128, 2 * cols], BF16))
        bia, bib, boa, bob, bv = dIa.n, dIb.n, dOa.n, dOb.n, sV.n
        NOPS = 6

        @blk.sync
        def _(q):
            for r in range(rows):
                if r >= 2:
                    q.wait_ge(sV.h, bv + NOPS * (r - 1))
                par = (dIa, dIb)[r % 2]
                par.inc(q.dma_start(out=wst[:, cols * (r % 2):cols * (r % 2 + 1)],
                                    in_=w_in[128 * r:128 * (r + 1), :]))
                if scr is not None:
                    q.wait_ge(sV.h, bv + NOPS * (r + 1))
                    opar = (dOa, dOb)[r % 2]
                    opar.inc(q.dma_start(out=scr[r, :, :],
                                         in_=wtb[:, cols * (r % 2):cols * (r % 2 + 1)]))

        @blk.vector
        def _(q):
            for r in range(rows):
                par, base = ((dIa, bia), (dIb, bib))[r % 2]
                q.wait_ge(par.h, 16 * (base + r // 2 + 1))
                if scr is not None and r >= 2:
                    opar, obase = ((dOa, boa), (dOb, bob))[r % 2]
                    q.wait_ge(opar.h, 16 * (obase + r // 2))
                src = wst[:, cols * (r % 2):cols * (r % 2 + 1)]
                dst = (wtb[:, cols * (r % 2):cols * (r % 2 + 1)] if scr is not None
                       else sbuf_dst(r))
                sV.inc(q.tensor_scalar(out=wtmp[:], in0=src, scalar1=rscol[:, 0:1],
                                       scalar2=None, op0=OP.mult))
                sV.inc(q.scalar_tensor_tensor(out=werr[:], in0=wtmp[:], scalar=nscol[:, 0:1],
                                              in1=src, op0=OP.mult, op1=OP.add))
                sV.inc(q.scalar_tensor_tensor(out=wtmp[:], in0=werr[:], scalar=rscol[:, 0:1],
                                              in1=wtmp[:], op0=OP.mult, op1=OP.add))
                sV.inc(q.tensor_scalar(out=wtmp[:], in0=wtmp[:], scalar1=1.0, scalar2=MAGIC,
                                       op0=OP.mult, op1=OP.add))
                sV.inc(q.tensor_scalar(out=wtmp[:], in0=wtmp[:], scalar1=MAGIC, scalar2=1.0,
                                       op0=OP.subtract, op1=OP.min))
                sV.inc(q.tensor_scalar(out=dst, in0=wtmp[:], scalar1=-1.0, scalar2=None,
                                       op0=OP.max))
        barrier()
        st.close()

    w_mean(w1t_in, NC_D, HID, smisc[:, 0:1], smisc[:, 1:2], smisc[:, 4:5])
    w_tern(w1t_in, NC_D, HID, smisc[:, 0:1], smisc[:, 1:2], smisc[:, 4:5], scr=scr_w1t)
    w_mean(w2t_in, NC_H, D, smisc[:, 2:3], smisc[:, 3:4], smisc[:, 5:6])

    seqws = ExitStack()
    q_ln = seqws.enter_context(nc.sbuf_tensor(_u("q_ln"), [128, NC_D * N], BF16))
    kt65 = seqws.enter_context(nc.sbuf_tensor(_u("kt65"), [65, NG * N], BF16))
    vaug = seqws.enter_context(nc.sbuf_tensor(_u("vaug"), [128, NG * 16 * 65], BF16))
    ao_b = seqws.enter_context(nc.sbuf_tensor(_u("ao_b"), [128, NC_D * N], BF16))
    qsq = seqws.enter_context(nc.sbuf_tensor(_u("qsq"), [128, N], F32))
    qt65 = [seqws.enter_context(nc.sbuf_tensor(f"qt65_{i}", [65, N], BF16)) for i in range(2)]
    mk2 = seqws.enter_context(nc.sbuf_tensor(_u("mk2"), [128, 8], F32))
    xstage = seqws.enter_context(nc.sbuf_tensor(_u("xstage"), [128, 2 * D], F32))
    xn0 = seqws.enter_context(nc.sbuf_tensor(_u("xn0"), [128, 2 * D], BF16))

    @blk.sync
    def _(q):
        for g in range(NG):
            dI.inc(q.dma_start(out=kt65[64:65, N * g:N * (g + 1)], in_=onesb_in))

    @blk.vector
    def _(q):
        q.memset(mk2[:], 0.0)
        q.memset(qsq[:], 0.0)
        sV.inc(q.memset(vaug[:], 1.0))
    barrier()

    def ln12_phase(b, xn_b):
        st = ExitStack()
        pstr = st.enter_context(nc.psum_tensor(_u("ps_tr12"), [128, 2 * 1024], BF16))
        bia, bib, bv, bp, ba = dIa.n, dIb.n, sV.n, sP.n, sA.n

        @blk.sync
        def _(q):
            for t in range(NTT):
                if t >= 2:
                    q.wait_ge(sV.h, bv + 4 * (t - 1))
                    q.wait_ge(sA.h, ba + 2 * (t - 1))
                par = (dIa, dIb)[t % 2]
                par.inc(q.dma_start(out=xstage[:, D * (t % 2):D * (t % 2 + 1)],
                                    in_=x_in[N * b + 128 * t: N * b + 128 * (t + 1), :]))

        @blk.scalar
        def _(q):
            for t in range(NTT):
                par, base = ((dIa, bia), (dIb, bib))[t % 2]
                q.wait_ge(par.h, 16 * (base + t // 2 + 1))
                if t >= 1:
                    q.wait_ge(sV.h, bv + 4 * (t - 1) + 2)
                xs = xstage[:, D * (t % 2):D * (t % 2 + 1)]
                AS(q, q.activation(out=qsq[:, 1024:2048], in_=xs, func=AF.Square,
                                   accum_out=colmisc[:, 1:2]))
                sA.inc(q.copy(out=colmisc[:, 13:14], in_=colmisc[:, 1:2]))
                q.wait_ge(sV.h, bv + 4 * t + 2)
                AS(q, q.activation(out=colmisc[:, 4:5], in_=colmisc[:, 3:4], func=AF.Sqrt))
                sA.inc(q.copy(out=colmisc[:, 13:14], in_=colmisc[:, 4:5]))

        @blk.vector
        def _(q):
            for t in range(NTT):
                par, base = ((dIa, bia), (dIb, bib))[t % 2]
                q.wait_ge(par.h, 16 * (base + t // 2 + 1))
                xs = xstage[:, D * (t % 2):D * (t % 2 + 1)]
                VS(q, q.tensor_scalar(out=qsq[:, 0:1024], in0=xs, scalar1=1.0, scalar2=0.0,
                                      op0=OP.mult, op1=OP.add, accum_out=colmisc[:, 0:1]))
                VS(q, q.tensor_scalar(out=colmisc[:, 0:1], in0=colmisc[:, 0:1],
                                      scalar1=float(1.0 / D), scalar2=None, op0=OP.mult))
                sV.inc(q.tensor_copy(out=colmisc[:, 15:16], in_=colmisc[:, 0:1]))
                VS(q, q.tensor_tensor(out=colmisc[:, 2:3], in0=colmisc[:, 0:1],
                                      in1=colmisc[:, 0:1], op=OP.mult))
                q.wait_ge(sA.h, ba + 2 * t + 1)
                VS(q, q.scalar_tensor_tensor(out=colmisc[:, 3:4], in0=colmisc[:, 1:2],
                                             scalar=float(1.0 / D), in1=colmisc[:, 2:3],
                                             op0=OP.mult, op1=OP.subtract))
                VS(q, q.tensor_scalar(out=colmisc[:, 3:4], in0=colmisc[:, 3:4],
                                      scalar1=float(1.0 + EPS2), scalar2=float(EPS1 * EPS2),
                                      op0=OP.mult, op1=OP.add))
                sV.inc(q.tensor_copy(out=colmisc[:, 15:16], in_=colmisc[:, 3:4]))
                q.wait_ge(sA.h, ba + 2 * t + 2)
                VS(q, q.reciprocal(out=colmisc[:, 5:6], in_=colmisc[:, 4:5]))
                sV.inc(q.tensor_scalar(out=xn0[:, D * (t % 2):D * (t % 2 + 1)], in0=xs,
                                       scalar1=colmisc[:, 0:1], scalar2=colmisc[:, 5:6],
                                       op0=OP.subtract, op1=OP.mult))
                q.wait_ge(sP.h, bp + t + 1)
                for c in range(NC_D):
                    pslc = pstr[:, 1024 * (t % 2) + 128 * c: 1024 * (t % 2) + 128 * (c + 1)]
                    ins = q.tensor_scalar(out=xn_b[:, N * c + 128 * t: N * c + 128 * (t + 1)],
                                          in0=pslc, scalar1=ln2g[:, c:c + 1],
                                          scalar2=ln2b[:, c:c + 1], op0=OP.mult, op1=OP.add)
                VS(q, ins)
                sV.inc(q.tensor_copy(out=colmisc[:, 15:16], in_=colmisc[:, 0:1]))

        @blk.tensor
        def _(q):
            for t in range(NTT):
                q.wait_ge(sV.h, bv + 4 * t + 3)
                if t >= 2:
                    q.wait_ge(sV.h, bv + 4 * (t - 1))
                for c in range(NC_D):
                    ins = q.transpose(
                        out=pstr[:, 1024 * (t % 2) + 128 * c: 1024 * (t % 2) + 128 * (c + 1)],
                        in_=xn0[:, D * (t % 2) + 128 * c: D * (t % 2) + 128 * (c + 1)],
                        identity=idb[:])
                sP.inc(ins)
        barrier()
        st.close()

    def proj_phase(b, xn_b):
        st = ExitStack()
        psq = st.enter_context(nc.psum_tensor(_u("ps_proj"), [128, 2 * 512], F32))
        bp, ba = sP.n, sA.n
        jobs = ([("q", m, tb) for m in range(NC_D) for tb in range(4)]
                + [("k", kc, tb) for kc in range(2) for tb in range(4)]
                + [("v", t, 0) for t in range(NTT)])

        @blk.tensor
        def _(q):
            for j, (kind, a1, a2) in enumerate(jobs):
                if j >= 2:
                    q.wait_ge(sA.h, ba + j - 1)
                half = 512 * (j % 2)
                if kind == "q":
                    m, tb = a1, a2
                    for ic in range(NC_D):
                        ins = q.matmul(out=psq[:, half:half + 512],
                                       lhsT=wq_b[:, D * ic + 128 * m: D * ic + 128 * (m + 1)],
                                       rhs=xn_b[:, N * ic + 512 * tb: N * ic + 512 * (tb + 1)],
                                       start=(ic == 0), stop=(ic == NC_D - 1))
                elif kind == "k":
                    kc, tb = a1, a2
                    for ic in range(NC_D):
                        ins = q.matmul(out=psq[:, half:half + 512],
                                       lhsT=wkv_b[:, 512 * ic + 128 * kc: 512 * ic + 128 * (kc + 1)],
                                       rhs=xn_b[:, N * ic + 512 * tb: N * ic + 512 * (tb + 1)],
                                       start=(ic == 0), stop=(ic == NC_D - 1))
                else:
                    t = a1
                    for ic in range(NC_D):
                        ins = q.matmul(out=psq[:, half:half + 256],
                                       lhsT=xn_b[:, N * ic + 128 * t: N * ic + 128 * (t + 1)],
                                       rhs=wkv_b[:, 512 * ic + 256: 512 * (ic + 1)],
                                       start=(ic == 0), stop=(ic == NC_D - 1))
                sP.inc(ins)

        @blk.scalar
        def _(q):
            for j, (kind, a1, a2) in enumerate(jobs):
                q.wait_ge(sP.h, bp + j + 1)
                half = 512 * (j % 2)
                if kind == "q":
                    m, tb = a1, a2
                    ins = q.copy(out=q_ln[:, N * m + 512 * tb: N * m + 512 * (tb + 1)],
                                 in_=psq[:, half:half + 512])
                elif kind == "k":
                    kc, tb = a1, a2
                    ins = q.copy(out=kraw[:, N * kc + 512 * tb: N * kc + 512 * (tb + 1)],
                                 in_=psq[:, half:half + 512])
                else:
                    t = a1
                    for g in range(NG):
                        ins = q.copy(out=vaug[:, 1040 * g + 65 * t: 1040 * g + 65 * t + 64],
                                     in_=psq[:, half + 64 * g: half + 64 * (g + 1)])
                sA.inc(ins)
        barrier()
        st.close()

    def headln_phase(b, nhd, ncnk, raw_get, bo_bf, bo_ff, sel, gcol, bcol, write_out):
        st = ExitStack()
        pst1 = st.enter_context(nc.psum_tensor(_u("ps_st1"), [nhd, 1024], F32))
        pst2 = st.enter_context(nc.psum_tensor(_u("ps_st2"), [nhd, 1024], F32))
        sums = st.enter_context(nc.sbuf_tensor(_u("hl_sums"), [nhd, N], F32))
        sumsq = st.enter_context(nc.sbuf_tensor(_u("hl_sumsq"), [nhd, N], F32))
        for H in range(2):
            ba, bp = sA.n, sP.n

            @blk.scalar
            def _(q, H=H):
                for c in range(ncnk):
                    q.wait_ge(sP.h, bp + c)
                    sA.inc(q.activation(out=qsq[:, 0:1024],
                                        in_=raw_get(c)[:, 1024 * H:1024 * (H + 1)],
                                        func=AF.Square))

            @blk.tensor
            def _(q, H=H):
                for c in range(ncnk):
                    q.wait_ge(sA.h, ba + c + 1)
                    for s in range(2):
                        q.matmul(out=pst1[:, 512 * s:512 * (s + 1)],
                                 lhsT=bo_bf[:, nhd * c:nhd * (c + 1)],
                                 rhs=raw_get(c)[:, 1024 * H + 512 * s:1024 * H + 512 * (s + 1)],
                                 start=(c == 0), stop=(c == ncnk - 1))
                    for s in range(2):
                        ins = q.matmul(out=pst2[:, 512 * s:512 * (s + 1)],
                                       lhsT=bo_ff[:, nhd * c:nhd * (c + 1)],
                                       rhs=qsq[:, 512 * s:512 * (s + 1)],
                                       start=(c == 0), stop=(c == ncnk - 1))
                    sP.inc(ins)

            @blk.vector
            def _(q, H=H, tgt=sP.n):
                q.wait_ge(sP.h, tgt)
                q.tensor_copy(out=sums[:, 1024 * H:1024 * (H + 1)], in_=pst1[:])
                sV.inc(q.tensor_copy(out=sumsq[:, 1024 * H:1024 * (H + 1)], in_=pst2[:]))

            @blk.tensor
            def _(q, tgt=sV.n):
                q.wait_ge(sV.h, tgt)
        bv, ba = sV.n, sA.n

        @blk.vector
        def _(q):
            q.tensor_scalar(out=sums[:], in0=sums[:], scalar1=float(1.0 / DH), scalar2=None,
                            op0=OP.mult)
            q.tensor_scalar(out=sumsq[:], in0=sumsq[:], scalar1=float(1.0 / DH), scalar2=None,
                            op0=OP.mult)
            q.tensor_tensor(out=qsq[0:nhd, 0:N], in0=sums[:], in1=sums[:], op=OP.mult)
            q.tensor_tensor(out=sumsq[:], in0=sumsq[:], in1=qsq[0:nhd, 0:N], op=OP.subtract)
            sV.inc(q.tensor_scalar(out=sumsq[:], in0=sumsq[:], scalar1=1.0, scalar2=float(EPS2),
                                   op0=OP.mult, op1=OP.add))

        @blk.scalar
        def _(q):
            q.wait_ge(sV.h, bv + 1)
            sA.inc(q.activation(out=sumsq[:], in_=sumsq[:], func=AF.Sqrt))

        @blk.vector
        def _(q):
            q.wait_ge(sA.h, ba + 1)
            VS(q, q.reciprocal(out=sumsq[:], in_=sumsq[:]))
            sV.inc(q.tensor_copy(out=qsq[0:nhd, 0:N], in_=sumsq[:]))
        barrier()

        st2 = ExitStack()
        psmu = st2.enter_context(nc.psum_tensor(_u("ps_mu"), [128, 1024], F32))
        psrs = st2.enter_context(nc.psum_tensor(_u("ps_rs"), [128, 1024], F32))
        qtmp = st2.enter_context(nc.sbuf_tensor(_u("hl_tmp"), [128, 1024], F32))
        for c in range(ncnk):
            for H in range(2):
                bp3 = sP.n

                @blk.tensor
                def _(q, c=c, H=H, bp3=bp3):
                    for s in range(2):
                        q.matmul(out=psmu[:, 512 * s:512 * (s + 1)],
                                 lhsT=sel[:, 128 * c:128 * (c + 1)],
                                 rhs=sums[:, 1024 * H + 512 * s:1024 * H + 512 * (s + 1)],
                                 start=True, stop=True)
                        ins = q.matmul(out=psrs[:, 512 * s:512 * (s + 1)],
                                       lhsT=sel[:, 128 * c:128 * (c + 1)],
                                       rhs=sumsq[:, 1024 * H + 512 * s:1024 * H + 512 * (s + 1)],
                                       start=True, stop=True)
                    sP.inc(ins)

                @blk.vector
                def _(q, c=c, H=H, tgt=sP.n):
                    q.wait_ge(sP.h, tgt)
                    q.tensor_tensor(out=qtmp[:], in0=raw_get(c)[:, 1024 * H:1024 * (H + 1)],
                                    in1=psmu[:], op=OP.subtract)
                    q.tensor_tensor(out=qtmp[:], in0=qtmp[:], in1=psrs[:], op=OP.mult)
                    write_out(q, c, H, qtmp)

                @blk.tensor
                def _(q, tgt=sV.n):
                    q.wait_ge(sV.h, tgt)
        barrier()
        st2.close()
        st.close()

    def headln_q(b):
        def write_out(q, c, H, qtmp):
            sl = slice(N * c + 1024 * H, N * c + 1024 * (H + 1))
            sV.inc(q.tensor_scalar(out=q_ln[:, sl], in0=qtmp[:],
                                   scalar1=qkn[:, 0:1], scalar2=qkn[:, 1:2],
                                   op0=OP.mult, op1=OP.add))
        headln_phase(b, 16, NC_D, lambda c: q_ln[:, N * c:N * (c + 1)],
                     boq_b, boq_f, sel16, qkn[:, 0:1], qkn[:, 1:2], write_out)

    def headln_k(b):
        def write_out(q, c, H, qtmp):
            gsl = slice(N * (2 * c) + 1024 * H, N * (2 * c) + 1024 * (H + 1))
            gsl2 = slice(N * (2 * c + 1) + 1024 * H, N * (2 * c + 1) + 1024 * (H + 1))
            q.tensor_scalar(out=kt65[0:64, gsl], in0=qtmp[0:64, :],
                            scalar1=qkn[0:64, 2:3], scalar2=qkn[0:64, 3:4],
                            op0=OP.mult, op1=OP.add)
            sV.inc(q.tensor_scalar(out=kt65[0:64, gsl2], in0=qtmp[64:128, :],
                                   scalar1=qkn[64:128, 2:3], scalar2=qkn[64:128, 3:4],
                                   op0=OP.mult, op1=OP.add))
        headln_phase(b, 4, 2, lambda c: kraw[:, N * c:N * (c + 1)],
                     bok_b, bok_f, selk, qkn[:, 2:3], qkn[:, 3:4], write_out)

    def mk2_phase(b):
        st = ExitStack()
        psn = st.enter_context(nc.psum_tensor(_u("ps_kn"), [65, 512], F32))
        for g in range(NG):
            ba4 = sA.n

            @blk.scalar
            def _(q, g=g, tgt=sP.n):
                q.wait_ge(sP.h, tgt)
                sA.inc(q.activation(out=qsq[0:64, 0:N], in_=kt65[0:64, N * g:N * (g + 1)],
                                    func=AF.Square))
            for quarter in range(4):
                bv4 = sV.n

                @blk.tensor
                def _(q, quarter=quarter, ba4=ba4, bv4=bv4):
                    q.wait_ge(sA.h, ba4 + 1)
                    if quarter >= 1:
                        q.wait_ge(sV.h, bv4)
                    sP.inc(q.matmul(out=psn[:], lhsT=bo65[0:64, 0:65],
                                    rhs=qsq[0:64, 512 * quarter:512 * (quarter + 1)],
                                    start=True, stop=True))

                @blk.vector
                def _(q, g=g, quarter=quarter, tgt=sP.n):
                    q.wait_ge(sP.h, tgt)
                    if quarter == 0:
                        VS(q, q.tensor_reduce(out=mk2[64:65, g:g + 1], in_=psn[64:65, :],
                                              axis=AX.X, op=OP.max))
                        sV.inc(q.tensor_copy(out=mk2[64:65, 6:7], in_=mk2[64:65, g:g + 1]))
                    else:
                        VS(q, q.tensor_reduce(out=mk2[64:65, 7:8], in_=psn[64:65, :],
                                              axis=AX.X, op=OP.max))
                        VS(q, q.tensor_tensor(out=mk2[64:65, g:g + 1], in0=mk2[64:65, g:g + 1],
                                              in1=mk2[64:65, 7:8], op=OP.max))
                        sV.inc(q.tensor_copy(out=mk2[64:65, 6:7], in_=mk2[64:65, g:g + 1]))

            @blk.vector
            def _(q, g=g):
                VS(q, q.tensor_scalar(out=mk2[64:65, g:g + 1], in0=mk2[64:65, g:g + 1],
                                      scalar1=64.0, scalar2=None, op0=OP.add))
                sV.inc(q.tensor_copy(out=mk2[64:65, 6:7], in_=mk2[64:65, g:g + 1]))
        barrier()
        st.close()

    def attn_core_phase(b, pslab, ao_tm):
        st = ExitStack()
        psS = [st.enter_context(nc.psum_tensor(_u(f"ps_S{i}"), [128, 1024], F32)) for i in range(2)]
        psPV = st.enter_context(nc.psum_tensor(_u("ps_PV"), [128, 65], F32))
        psn = st.enter_context(nc.psum_tensor(_u("ps_qn"), [65, 512], F32))
        psTR = st.enter_context(nc.psum_tensor(_u("ps_aTR"), [128, 1024], BF16))

        for c in range(NC_D):
            g = c // 2
            ba = sA.n

            @blk.scalar
            def _(q, c=c, tgt=sP.n):
                q.wait_ge(sP.h, tgt)
                sA.inc(q.activation(out=qsq[:], in_=q_ln[:, N * c:N * (c + 1)], func=AF.Square))

            for hp in range(2):
                hb = hp

                @blk.vector
                def _(q, c=c, hp=hp, hb=hb, tgt=sP.n):
                    q.wait_ge(sP.h, tgt)
                    sV.inc(q.tensor_scalar(out=qt65[hb][0:64, :],
                                           in0=q_ln[64 * hp:64 * (hp + 1), N * c:N * (c + 1)],
                                           scalar1=0.125, scalar2=None, op0=OP.mult))
                for quarter in range(4):
                    bv4 = sV.n

                    @blk.tensor
                    def _(q, hp=hp, quarter=quarter, ba=ba, bv4=bv4):
                        q.wait_ge(sA.h, ba + 1)
                        if quarter >= 1:
                            q.wait_ge(sV.h, bv4)
                        sP.inc(q.matmul(out=psn[:], lhsT=bo65[:, 65 * hp:65 * (hp + 1)],
                                        rhs=qsq[:, 512 * quarter:512 * (quarter + 1)],
                                        start=True, stop=True))

                    @blk.vector
                    def _(q, g=g, hb=hb, quarter=quarter, tgt=sP.n):
                        q.wait_ge(sP.h, tgt)
                        sV.inc(q.tensor_scalar(
                            out=qt65[hb][64:65, 512 * quarter:512 * (quarter + 1)],
                            in0=psn[64:65, :], scalar1=mk2[64:65, g:g + 1],
                            scalar2=float(-1.0 / 16.0), op0=OP.add, op1=OP.mult))
                bvq = sV.n

                for qb in range(2):
                    bpS, baE = sP.n, sA.n

                    @blk.tensor
                    def _(q, g=g, hb=hb, qb=qb, bvq=bvq, baE=baE):
                        q.wait_ge(sV.h, bvq)
                        for kt in range(16):
                            if kt >= 2:
                                q.wait_ge(sA.h, baE + kt - 1)
                            for s in range(2):
                                ins = q.matmul(
                                    out=psS[kt % 2][:, 512 * s:512 * (s + 1)],
                                    lhsT=kt65[:, N * g + 128 * kt: N * g + 128 * (kt + 1)],
                                    rhs=qt65[hb][:, 1024 * qb + 512 * s: 1024 * qb + 512 * (s + 1)],
                                    start=True, stop=True)
                            sP.inc(ins)

                    @blk.scalar
                    def _(q, bpS=bpS):
                        for kt in range(16):
                            q.wait_ge(sP.h, bpS + kt + 1)
                            sA.inc(q.activation(out=pslab[:, 1024 * kt:1024 * (kt + 1)],
                                                in_=psS[kt % 2][:], func=AF.Exp))
                    baE2 = sA.n

                    for qt in range(8):
                        @blk.tensor
                        def _(q, g=g, qt=qt, baE2=baE2, tgtV=sV.n):
                            q.wait_ge(sA.h, baE2)
                            if qt >= 1:
                                q.wait_ge(sV.h, tgtV)
                            for kc in range(16):
                                ins = q.matmul(
                                    out=psPV[:],
                                    lhsT=pslab[:, 1024 * kc + 128 * qt: 1024 * kc + 128 * (qt + 1)],
                                    rhs=vaug[:, 1040 * g + 65 * kc: 1040 * g + 65 * (kc + 1)],
                                    start=(kc == 0), stop=(kc == 15))
                            sP.inc(ins)

                        @blk.vector
                        def _(q, hp=hp, qb=qb, qt=qt, tgt=sP.n):
                            q.wait_ge(sP.h, tgt)
                            VS(q, q.reciprocal(out=colmisc[:, 8:9], in_=psPV[:, 64:65]))
                            qtg = 8 * qb + qt
                            sV.inc(q.tensor_scalar(
                                out=ao_tm[:, 128 * qtg + 64 * hp: 128 * qtg + 64 * (hp + 1)],
                                in0=psPV[:, 0:64], scalar1=colmisc[:, 8:9],
                                scalar2=None, op0=OP.mult))

            for half in range(2):
                @blk.tensor
                def _(q, half=half, tgt=sV.n):
                    q.wait_ge(sV.h, tgt)
                    for j in range(8):
                        qtg = 8 * half + j
                        ins = q.transpose(out=psTR[:, 128 * j:128 * (j + 1)],
                                          in_=ao_tm[:, 128 * qtg:128 * (qtg + 1)],
                                          identity=idb[:])
                    sP.inc(ins)

                @blk.vector
                def _(q, c=c, half=half, tgt=sP.n):
                    q.wait_ge(sP.h, tgt)
                    sV.inc(q.tensor_copy(
                        out=ao_b[:, N * c + 1024 * half: N * c + 1024 * (half + 1)],
                        in_=psTR[:]))

                @blk.tensor
                def _(q, tgt=sV.n):
                    q.wait_ge(sV.h, tgt)
        barrier()
        st.close()

    def wout_phase(b):
        st = ExitStack()
        psW = st.enter_context(nc.psum_tensor(_u("ps_W"), [128, 2 * 512], F32))
        psT2 = st.enter_context(nc.psum_tensor(_u("ps_WT"), [128, 512], F32))
        wsc = st.enter_context(nc.sbuf_tensor(_u("wsc"), [128, 2 * 512], F32))
        x2asm = st.enter_context(nc.sbuf_tensor(_u("x2asm"), [128, 4 * 1024], F32))
        x2r = x2asm[:].rearrange("p (j w) -> p j w", w=1024)
        for tb in range(4):
            bp, ba, bv = sP.n, sA.n, sV.n
            for dc in range(NC_D):
                @blk.tensor
                def _(q, tb=tb, dc=dc, ba=ba):
                    if dc >= 2:
                        q.wait_ge(sA.h, ba + dc - 1)
                    half = 512 * (dc % 2)
                    for ic in range(NC_D):
                        ins = q.matmul(out=psW[:, half:half + 512],
                                       lhsT=wout_b[:, D * ic + 128 * dc: D * ic + 128 * (dc + 1)],
                                       rhs=ao_b[:, N * ic + 512 * tb: N * ic + 512 * (tb + 1)],
                                       start=(ic == 0), stop=(ic == NC_D - 1))
                    sP.inc(ins)

                @blk.scalar
                def _(q, dc=dc, tgt=sP.n):
                    q.wait_ge(sP.h, tgt)
                    half = 512 * (dc % 2)
                    sA.inc(q.copy(out=wsc[:, half:half + 512], in_=psW[:, half:half + 512]))

                @blk.tensor
                def _(q, dc=dc, tgt=sA.n, bv=bv):
                    q.wait_ge(sA.h, tgt)
                    if dc >= 1:
                        q.wait_ge(sV.h, bv + dc)
                    half = 512 * (dc % 2)
                    for j in range(4):
                        ins = q.transpose(out=psT2[:, 128 * j:128 * (j + 1)],
                                          in_=wsc[:, half + 128 * j: half + 128 * (j + 1)],
                                          identity=idf[:])
                    sP.inc(ins)

                @blk.vector
                def _(q, dc=dc, tgt=sP.n):
                    q.wait_ge(sP.h, tgt)
                    sV.inc(q.tensor_copy(
                        out=x2r[:, :, 128 * dc:128 * (dc + 1)],
                        in_=psT2[:].rearrange("p (j w) -> p j w", w=128)))
            bvr = sV.n
            bira, birb = dIa.n, dIb.n

            @blk.sync
            def _(q, b=b, tb=tb, bvr=bvr):
                for j in range(4):
                    t = 4 * tb + j
                    par = (dIa, dIb)[j % 2]
                    par.inc(q.dma_start(out=xstage[:, D * (j % 2):D * (j % 2 + 1)],
                                        in_=x_in[N * b + 128 * t: N * b + 128 * (t + 1), :]))
                    q.wait_ge(sV.h, bvr + j + 1)
                    dO.inc(q.dma_start(out=scr_x2[N * b + 128 * t: N * b + 128 * (t + 1), :],
                                       in_=x2asm[:, 1024 * j:1024 * (j + 1)]))

            @blk.vector
            def _(q, tb=tb, bira=bira, birb=birb):
                for j in range(4):
                    par, base = ((dIa, bira), (dIb, birb))[j % 2]
                    q.wait_ge(par.h, 16 * (base + j // 2 + 1))
                    sV.inc(q.tensor_tensor(out=x2asm[:, 1024 * j:1024 * (j + 1)],
                                           in0=x2asm[:, 1024 * j:1024 * (j + 1)],
                                           in1=xstage[:, D * (j % 2):D * (j % 2 + 1)],
                                           op=OP.add))
            barrier()
        st.close()

    kraw = None
    for b in range(NSEQ):
        krsc = ExitStack()
        kraw = krsc.enter_context(nc.sbuf_tensor(_u("kraw"), [128, 2 * N], BF16))
        xnsc = ExitStack()
        xn_b = xnsc.enter_context(nc.sbuf_tensor(_u("xn_b"), [128, NC_D * N], BF16))
        ln12_phase(b, xn_b)
        if b == 0:
            dbg_dump("xn", xn_b[:])
        proj_phase(b, xn_b)
        xnsc.close()
        if b == 0:
            dbg_dump("qraw", q_ln[:])
            dbg_dump("kraw", kraw[:])
        headln_q(b)
        headln_k(b)
        mk2_phase(b)
        krsc.close()
        if b == 0:
            dbg_dump("qln", q_ln[:])
            dbg_dump("kt65", kt65[:])
            dbg_dump("vaug", vaug[:])
        core_sc = ExitStack()
        pslab = core_sc.enter_context(nc.sbuf_tensor(_u("pslab"), [128, 16 * 1024], BF16))
        ao_tm = core_sc.enter_context(nc.sbuf_tensor(_u("ao_tm"), [128, 16 * 128], BF16))
        attn_core_phase(b, pslab, ao_tm)
        if b == 0:
            dbg_dump("ao", ao_b[:])
        core_sc.close()
        wout_phase(b)

    dbg_dump("x2", scr_x2)
    seqws.close()
    awts.close()

    ffn = ExitStack()
    yi_fm = ffn.enter_context(nc.sbuf_tensor(_u("yi_fm"), [128, NC_D * NSEQ * N], BF16))
    NT_ALL = NSEQ * NTT

    def ln3_quant_phase():
        st = ExitStack()
        pstr = st.enter_context(nc.psum_tensor(_u("ps_tr3"), [128, 2 * 1024], BF16))
        xst = st.enter_context(nc.sbuf_tensor(_u("f_xst"), [128, 2 * D], F32))
        ytmp = st.enter_context(nc.sbuf_tensor(_u("f_ytmp"), [128, D], F32))
        yjunk = st.enter_context(nc.sbuf_tensor(_u("f_yjunk"), [128, 2 * D], F32))
        yib = st.enter_context(nc.sbuf_tensor(_u("f_yib"), [128, 2 * D], BF16))
        bia, bib, bv, bp, ba = dIa.n, dIb.n, sV.n, sP.n, sA.n
        yir = yi_fm[:].rearrange("p (c w) -> p c w", w=NSEQ * N)

        @blk.sync
        def _(q):
            for t in range(NT_ALL):
                if t >= 2:
                    q.wait_ge(sV.h, bv + 6 * (t - 1))
                    q.wait_ge(sA.h, ba + 2 * (t - 1))
                par = (dIa, dIb)[t % 2]
                par.inc(q.dma_start(out=xst[:, D * (t % 2):D * (t % 2 + 1)],
                                   in_=scr_x2[128 * t:128 * (t + 1), :]))

        @blk.scalar
        def _(q):
            for t in range(NT_ALL):
                par, base = ((dIa, bia), (dIb, bib))[t % 2]
                q.wait_ge(par.h, 16 * (base + t // 2 + 1))
                if t >= 1:
                    q.wait_ge(sV.h, bv + 6 * (t - 1) + 2)
                xs = xst[:, D * (t % 2):D * (t % 2 + 1)]
                AS(q, q.activation(out=yjunk[:, D:2 * D], in_=xs, func=AF.Square,
                                   accum_out=colmisc[:, 1:2]))
                sA.inc(q.copy(out=colmisc[:, 13:14], in_=colmisc[:, 1:2]))
                q.wait_ge(sV.h, bv + 6 * t + 2)
                AS(q, q.activation(out=colmisc[:, 4:5], in_=colmisc[:, 3:4], func=AF.Sqrt))
                sA.inc(q.copy(out=colmisc[:, 13:14], in_=colmisc[:, 4:5]))

        @blk.vector
        def _(q):
            for t in range(NT_ALL):
                par, base = ((dIa, bia), (dIb, bib))[t % 2]
                q.wait_ge(par.h, 16 * (base + t // 2 + 1))
                xs = xst[:, D * (t % 2):D * (t % 2 + 1)]
                VS(q, q.tensor_scalar(out=yjunk[:, 0:D], in0=xs, scalar1=1.0, scalar2=0.0,
                                      op0=OP.mult, op1=OP.add, accum_out=colmisc[:, 0:1]))
                VS(q, q.tensor_scalar(out=colmisc[:, 0:1], in0=colmisc[:, 0:1],
                                      scalar1=float(1.0 / D), scalar2=None, op0=OP.mult))
                sV.inc(q.tensor_copy(out=colmisc[:, 15:16], in_=colmisc[:, 0:1]))
                VS(q, q.tensor_tensor(out=colmisc[:, 2:3], in0=colmisc[:, 0:1],
                                      in1=colmisc[:, 0:1], op=OP.mult))
                q.wait_ge(sA.h, ba + 2 * t + 1)
                VS(q, q.scalar_tensor_tensor(out=colmisc[:, 3:4], in0=colmisc[:, 1:2],
                                             scalar=float(1.0 / D), in1=colmisc[:, 2:3],
                                             op0=OP.mult, op1=OP.subtract))
                VS(q, q.tensor_scalar(out=colmisc[:, 3:4], in0=colmisc[:, 3:4],
                                      scalar1=1.0, scalar2=float(EPS1),
                                      op0=OP.mult, op1=OP.add))
                sV.inc(q.tensor_copy(out=colmisc[:, 15:16], in_=colmisc[:, 3:4]))
                q.wait_ge(sA.h, ba + 2 * t + 2)
                VS(q, q.reciprocal(out=colmisc[:, 5:6], in_=colmisc[:, 4:5]))
                sV.inc(q.tensor_scalar(out=ytmp[:], in0=xs,
                                       scalar1=colmisc[:, 0:1], scalar2=None, op0=OP.subtract))
                VS(q, q.tensor_reduce(out=colmisc[:, 6:7], in_=ytmp[:], axis=AX.X, op=OP.max,
                                      apply_absolute_value=True))
                VS(q, q.tensor_tensor(out=colmisc[:, 6:7], in0=colmisc[:, 6:7],
                                      in1=colmisc[:, 5:6], op=OP.mult))
                VS(q, q.tensor_scalar(out=colmisc[:, 6:7], in0=colmisc[:, 6:7],
                                      scalar1=float(EPSQ), scalar2=None, op0=OP.max))
                q.scalar_tensor_tensor(out=c1c[:, t:t + 1], in0=colmisc[:, 6:7],
                                       scalar=float(1.0 / 127.0), in1=smisc[:, 0:1],
                                       op0=OP.mult, op1=OP.mult)
                VS(q, q.reciprocal(out=colmisc[:, 7:8], in_=colmisc[:, 6:7]))
                VS(q, q.tensor_scalar(out=colmisc[:, 7:8], in0=colmisc[:, 7:8],
                                      scalar1=127.0, scalar2=None, op0=OP.mult))
                VS(q, q.tensor_tensor(out=colmisc[:, 7:8], in0=colmisc[:, 7:8],
                                      in1=colmisc[:, 5:6], op=OP.mult))
                sV.inc(q.tensor_copy(out=colmisc[:, 15:16], in_=colmisc[:, 7:8]))
                q.tensor_scalar(out=ytmp[:], in0=ytmp[:], scalar1=colmisc[:, 7:8],
                                scalar2=MAGIC, op0=OP.mult, op1=OP.add)
                sV.inc(q.tensor_scalar(out=yib[:, D * (t % 2):D * (t % 2 + 1)], in0=ytmp[:],
                                       scalar1=MAGIC, scalar2=None, op0=OP.subtract))
                q.wait_ge(sP.h, bp + t + 1)
                ins = q.tensor_copy(out=yir[:, :, 128 * t:128 * (t + 1)],
                                    in_=pstr[:, 1024 * (t % 2):1024 * (t % 2 + 1)]
                                    .rearrange("p (c w) -> p c w", w=128))
                VS(q, ins)
                sV.inc(q.tensor_copy(out=colmisc[:, 15:16], in_=colmisc[:, 0:1]))

        @blk.tensor
        def _(q):
            for t in range(NT_ALL):
                q.wait_ge(sV.h, bv + 6 * t + 5)
                if t >= 2:
                    q.wait_ge(sV.h, bv + 6 * (t - 1))
                for c in range(NC_D):
                    ins = q.transpose(
                        out=pstr[:, 1024 * (t % 2) + 128 * c: 1024 * (t % 2) + 128 * (c + 1)],
                        in_=yib[:, D * (t % 2) + 128 * c: D * (t % 2) + 128 * (c + 1)],
                        identity=idb[:])
                sP.inc(ins)
        barrier()
        st.close()

    def ffn1_phase():
        st = ExitStack()
        psF = st.enter_context(nc.psum_tensor(_u("ps_F1"), [128, 2 * 512], F32))
        pstr = st.enter_context(nc.psum_tensor(_u("ps_trH"), [128, 1024], BF16))
        w1tb = st.enter_context(nc.sbuf_tensor(_u("w1tb"), [128, 2 * HID], BF16))
        htile = st.enter_context(nc.sbuf_tensor(_u("htile"), [128, 4 * HID], F32))
        hqtm = st.enter_context(nc.sbuf_tensor(_u("hqtm"), [128, HID], BF16))
        hqfm = st.enter_context(nc.sbuf_tensor(_u("hqfm"), [128, NC_H * 512], BF16))
        hqr = hqfm[:].rearrange("p (c w) -> p c w", w=512)
        for bl in range(8):
            bia, bib, bv, bp, ba = dIa.n, dIb.n, sV.n, sP.n, sA.n
            for hb in range(8):
                buf = hb % 2

                @blk.sync
                def _(q, bl=bl, hb=hb, buf=buf, bp=bp):
                    if hb >= 2:
                        q.wait_ge(sP.h, bp + 4 * (hb - 1))
                    par = (dIa, dIb)[hb % 2]
                    for ic in range(NC_D):
                        par.inc(q.dma_start(
                            out=w1tb[:, HID * buf + 512 * ic: HID * buf + 512 * (ic + 1)],
                            in_=scr_w1t[ic, :, 512 * hb:512 * (hb + 1)]))

                for tt in range(4):
                    @blk.tensor
                    def _(q, bl=bl, hb=hb, tt=tt, buf=buf, bia=bia, bib=bib, bv=bv):
                        par, base = ((dIa, bia), (dIb, bib))[hb % 2]
                        q.wait_ge(par.h, 16 * (base + 8 * (hb // 2 + 1)))
                        jj = 4 * hb + tt
                        if jj >= 2:
                            q.wait_ge(sV.h, bv + jj - 1)
                        half = 512 * (jj % 2)
                        tg = 4 * bl + tt
                        for ic in range(NC_D):
                            ins = q.matmul(
                                out=psF[:, half:half + 512],
                                lhsT=yi_fm[:, NSEQ * N * ic + 128 * tg: NSEQ * N * ic + 128 * (tg + 1)],
                                rhs=w1tb[:, HID * buf + 512 * ic: HID * buf + 512 * (ic + 1)],
                                start=(ic == 0), stop=(ic == NC_D - 1))
                        sP.inc(ins)

                    @blk.vector
                    def _(q, bl=bl, hb=hb, tt=tt, tgt=sP.n):
                        q.wait_ge(sP.h, tgt)
                        jj = 4 * hb + tt
                        half = 512 * (jj % 2)
                        tg = 4 * bl + tt
                        sV.inc(q.tensor_scalar(
                            out=htile[:, HID * tt + 512 * hb: HID * tt + 512 * (hb + 1)],
                            in0=psF[:, half:half + 512], scalar1=c1c[:, tg:tg + 1],
                            scalar2=None, op0=OP.mult))
            for tt in range(4):
                tg = 4 * bl + tt
                bvg = sV.n

                @blk.scalar
                def _(q, tt=tt, tgt=sV.n):
                    q.wait_ge(sV.h, tgt)
                    sA.inc(q.activation(out=htile[:, HID * tt:HID * (tt + 1)],
                                        in_=htile[:, HID * tt:HID * (tt + 1)], func=AF.Gelu))

                @blk.vector
                def _(q, tt=tt, tg=tg, tgt=sA.n):
                    q.wait_ge(sA.h, tgt)
                    ht = htile[:, HID * tt:HID * (tt + 1)]
                    VS(q, q.tensor_reduce(out=colmisc[:, 6:7], in_=ht, axis=AX.X, op=OP.max,
                                          apply_absolute_value=True))
                    VS(q, q.tensor_scalar(out=colmisc[:, 6:7], in0=colmisc[:, 6:7],
                                          scalar1=float(EPSQ), scalar2=None, op0=OP.max))
                    q.scalar_tensor_tensor(out=c2c[:, tg:tg + 1], in0=colmisc[:, 6:7],
                                           scalar=float(1.0 / 127.0), in1=smisc[:, 2:3],
                                           op0=OP.mult, op1=OP.mult)
                    VS(q, q.reciprocal(out=colmisc[:, 7:8], in_=colmisc[:, 6:7]))
                    VS(q, q.tensor_scalar(out=colmisc[:, 7:8], in0=colmisc[:, 7:8],
                                          scalar1=127.0, scalar2=None, op0=OP.mult))
                    q.tensor_scalar(out=ht, in0=ht, scalar1=colmisc[:, 7:8],
                                    scalar2=MAGIC, op0=OP.mult, op1=OP.add)
                    sV.inc(q.tensor_scalar(out=hqtm[:], in0=ht, scalar1=MAGIC,
                                           scalar2=None, op0=OP.subtract))
                for rnd in range(4):
                    @blk.tensor
                    def _(q, tt=tt, rnd=rnd, tgt=sV.n):
                        q.wait_ge(sV.h, tgt)
                        for j in range(8):
                            hc = 8 * rnd + j
                            ins = q.transpose(out=pstr[:, 128 * j:128 * (j + 1)],
                                              in_=hqtm[:, 128 * hc:128 * (hc + 1)],
                                              identity=idb[:])
                        sP.inc(ins)

                    @blk.vector
                    def _(q, tt=tt, rnd=rnd, tgt=sP.n):
                        q.wait_ge(sP.h, tgt)
                        sV.inc(q.tensor_copy(
                            out=hqr[:, 8 * rnd:8 * (rnd + 1), 128 * tt:128 * (tt + 1)],
                            in_=pstr[:].rearrange("p (c w) -> p c w", w=128)))

                    @blk.tensor
                    def _(q, tgt=sV.n):
                        q.wait_ge(sV.h, tgt)
            bvS = sV.n

            @blk.sync
            def _(q, bl=bl, bvS=bvS):
                q.wait_ge(sV.h, bvS)
                for hc in range(NC_H):
                    dO.inc(q.dma_start(out=scr_hq[hc, :, 512 * bl:512 * (bl + 1)],
                                       in_=hqfm[:, 512 * hc:512 * (hc + 1)]))
            to = dO.n

            @blk.vector
            def _(q, to=to):
                q.wait_ge(dO.h, 16 * to)
        barrier()
        st.close()

    def ffn2_phase():
        st = ExitStack()
        w2tb = st.enter_context(nc.sbuf_tensor(_u("w2tb"), [128, NC_H * 1024], BF16))
        w_tern(w2t_in, NC_H, D, smisc[:, 2:3], smisc[:, 3:4], smisc[:, 5:6],
               sbuf_dst=lambda r: w2tb[:, 1024 * r:1024 * (r + 1)])

        psF = st.enter_context(nc.psum_tensor(_u("ps_F2"), [128, 2 * 512], F32))
        psT2 = st.enter_context(nc.psum_tensor(_u("ps_F2T"), [128, 512], F32))
        hqin = st.enter_context(nc.sbuf_tensor(_u("hqin"), [128, NC_H * 512], BF16))
        wsc = st.enter_context(nc.sbuf_tensor(_u("f2wsc"), [128, 2 * 512], F32))
        oasm = st.enter_context(nc.sbuf_tensor(_u("oasm"), [128, 4 * 1024], F32))
        xst = st.enter_context(nc.sbuf_tensor(_u("f2xst"), [128, 2 * D], F32))
        oar = oasm[:].rearrange("p (j w) -> p j w", w=1024)
        for bl in range(8):
            bi = dI.n

            @blk.sync
            def _(q, bl=bl):
                for hc in range(NC_H):
                    dI.inc(q.dma_start(out=hqin[:, 512 * hc:512 * (hc + 1)],
                                       in_=scr_hq[hc, :, 512 * bl:512 * (bl + 1)]))
            bp, ba, bv = sP.n, sA.n, sV.n
            for dc in range(NC_D):
                @blk.tensor
                def _(q, dc=dc, bi=bi, ba=ba):
                    q.wait_ge(dI.h, 16 * (bi + NC_H))
                    if dc >= 2:
                        q.wait_ge(sA.h, ba + dc - 1)
                    half = 512 * (dc % 2)
                    for hc in range(NC_H):
                        ins = q.matmul(out=psF[:, half:half + 512],
                                       lhsT=w2tb[:, 1024 * hc + 128 * dc: 1024 * hc + 128 * (dc + 1)],
                                       rhs=hqin[:, 512 * hc:512 * (hc + 1)],
                                       start=(hc == 0), stop=(hc == NC_H - 1))
                    sP.inc(ins)

                @blk.scalar
                def _(q, dc=dc, tgt=sP.n):
                    q.wait_ge(sP.h, tgt)
                    half = 512 * (dc % 2)
                    sA.inc(q.copy(out=wsc[:, half:half + 512], in_=psF[:, half:half + 512]))

                @blk.tensor
                def _(q, dc=dc, tgt=sA.n, bv=bv):
                    q.wait_ge(sA.h, tgt)
                    if dc >= 1:
                        q.wait_ge(sV.h, bv + dc)
                    half = 512 * (dc % 2)
                    for j in range(4):
                        ins = q.transpose(out=psT2[:, 128 * j:128 * (j + 1)],
                                          in_=wsc[:, half + 128 * j: half + 128 * (j + 1)],
                                          identity=idf[:])
                    sP.inc(ins)

                @blk.vector
                def _(q, dc=dc, tgt=sP.n):
                    q.wait_ge(sP.h, tgt)
                    sV.inc(q.tensor_copy(
                        out=oar[:, :, 128 * dc:128 * (dc + 1)],
                        in_=psT2[:].rearrange("p (j w) -> p j w", w=128)))
            bvr = sV.n
            bira, birb = dIa.n, dIb.n

            @blk.sync
            def _(q, bl=bl, bvr=bvr):
                for j in range(4):
                    t = 4 * bl + j
                    par = (dIa, dIb)[j % 2]
                    par.inc(q.dma_start(out=xst[:, D * (j % 2):D * (j % 2 + 1)],
                                        in_=scr_x2[128 * t:128 * (t + 1), :]))
                    q.wait_ge(sV.h, bvr + j + 1)
                    dO.inc(q.dma_start(out=y_out[128 * t:128 * (t + 1), :],
                                       in_=oasm[:, 1024 * j:1024 * (j + 1)]))

            @blk.vector
            def _(q, bl=bl, bira=bira, birb=birb):
                for j in range(4):
                    t = 4 * bl + j
                    par, base = ((dIa, bira), (dIb, birb))[j % 2]
                    q.wait_ge(par.h, 16 * (base + j // 2 + 1))
                    sV.inc(q.scalar_tensor_tensor(out=oasm[:, 1024 * j:1024 * (j + 1)],
                                                  in0=oasm[:, 1024 * j:1024 * (j + 1)],
                                                  scalar=c2c[:, t:t + 1],
                                                  in1=xst[:, D * (j % 2):D * (j % 2 + 1)],
                                                  op0=OP.mult, op1=OP.add))
            barrier()
        st.close()

    ln3_quant_phase()
    dbg_dump("yifm", yi_fm[:])
    dbg_dump("c1c", c1c[:])
    dbg_dump("s12", smisc[:])
    ffn1_phase()
    dbg_dump("hq", scr_hq)
    dbg_dump("c2c", c2c[:])
    ffn2_phase()
    ffn.close()

    dvals = [(c.h, 16 * c.n) for c in DSEMS]

    @blk.sync
    def _(q):
        for h, v in dvals:
            q.wait_ge(h, v)

    es.close()
    return nc, DBG


_ID128F = np.eye(128, dtype=np.float32)


def _consts():
    c = {}
    c["idf"] = _ID128F
    c["idb"] = _ID128F.astype(ml_dtypes.bfloat16)
    c["onesb"] = np.ones((1, N), dtype=ml_dtypes.bfloat16)
    c["onesrowf"] = np.ones((1, 128), dtype=np.float32)
    bo65 = np.zeros((128, 130), dtype=np.float32)
    bo65[0:64, 64] = 1.0
    bo65[64:128, 65 + 64] = 1.0
    c["bo65"] = bo65
    boq = np.zeros((128, 128), dtype=np.float32)
    for ch in range(8):
        for p in range(128):
            boq[p, 16 * ch + 2 * ch + p // 64] = 1.0
    c["boq_f"] = boq
    c["boq_b"] = boq.astype(ml_dtypes.bfloat16)
    bok = np.zeros((128, 8), dtype=np.float32)
    for ch in range(2):
        for p in range(128):
            bok[p, 4 * ch + 2 * ch + p // 64] = 1.0
    c["bok_f"] = bok
    c["bok_b"] = bok.astype(ml_dtypes.bfloat16)
    sel16 = np.zeros((16, 1024), dtype=np.float32)
    for ch in range(8):
        for p in range(128):
            sel16[2 * ch + p // 64, 128 * ch + p] = 1.0
    c["sel16"] = sel16
    selk = np.zeros((4, 256), dtype=np.float32)
    for ch in range(2):
        for p in range(128):
            selk[2 * ch + p // 64, 128 * ch + p] = 1.0
    c["selk"] = selk
    c["onescolb"] = np.ones((128, 1), dtype=ml_dtypes.bfloat16)
    c["onescolf"] = np.ones((128, 1), dtype=np.float32)
    return c


def make_inmaps(inputs):
    L = inputs
    x = np.ascontiguousarray(np.asarray(L["L0_attn_x"], dtype=np.float32))
    consts = _consts()
    shared = dict(
        wq=np.ascontiguousarray(np.asarray(L["L0_attn_Wq"], np.float32)),
        wkv=np.ascontiguousarray(np.asarray(L["L0_attn_Wkv"], np.float32)),
        wout=np.ascontiguousarray(np.asarray(L["L0_attn_Wout"], np.float32)),
        w1t=np.ascontiguousarray(np.asarray(L["L0_ffn_w1"], np.float32).T),
        w2t=np.ascontiguousarray(np.asarray(L["L0_ffn_w2"], np.float32).T),
        ln2g=np.ascontiguousarray(np.asarray(L["L0_attn_ln_scale"], np.float32).reshape(8, 128).T),
        ln2b=np.ascontiguousarray(np.asarray(L["L0_attn_ln_bias"], np.float32).reshape(8, 128).T),
        qkn=np.ascontiguousarray(np.stack([
            np.tile(np.asarray(L["L0_attn_qn_scale"], np.float32), 2),
            np.tile(np.asarray(L["L0_attn_qn_bias"], np.float32), 2),
            np.tile(np.asarray(L["L0_attn_kn_scale"], np.float32), 2),
            np.tile(np.asarray(L["L0_attn_kn_bias"], np.float32), 2)], axis=1)),
        **consts,
    )
    maps = []
    for core in range(8):
        m = dict(shared)
        m["x"] = np.ascontiguousarray(x[2 * core:2 * core + 2].reshape(NSEQ * N, D))
        maps.append(m)
    return maps


def kernel(**inputs):
    nc, _ = build_program()
    maps = make_inmaps(inputs)
    res = run_bass_kernel_spmd(nc, maps, core_ids=list(range(8)))
    out = np.empty((16, N, D), dtype=np.float32)
    for core in range(8):
        out[2 * core:2 * core + 2] = res.results[core]["y"].reshape(NSEQ, N, D)
    return out
